# version 26
# baseline (speedup 1.0000x reference)
import math
import os
from contextlib import ExitStack

os.environ.setdefault("TRN_ALLOW_ICI_ALL_TO_ALL", "1")

import numpy as np
import concourse.bass as bass
import concourse.mybir as mybir
from concourse.bass_utils import run_bass_kernel_spmd

F32 = mybir.dt.float32
BF16 = mybir.dt.bfloat16
I32 = mybir.dt.int32
AF = mybir.ActivationFunctionType
ALU = mybir.AluOpType
AX = mybir.AxisListType.X

NCORE = 8
D = 2048
S = 8192
T = S // NCORE
NKC = D // 128
E = 8
M = 1024
NH = 16
HPC = 2
DH = 128
DR = 64
DCKV = 512
DCQ = 1024
SCALE = (DH + DR) ** -0.5
EPS = 1e-5
NEGBIG = -1.0e30
TWO_PI = 2.0 * math.pi
CLOG = math.log(1.02 * SCALE)

ENGS = ("tensor", "vector", "scalar", "gpsimd", "sync")


class Prog:

    COMPUTE_SEMS = ("pe", "vec", "act", "pool", "cc", "bar")

    def __init__(self, nc, stack, dma_streams):
        self.nc = nc
        self.rec = {e: [] for e in ENGS}
        self.sem = {}
        self.cnt = {}
        self.seen = {e: {} for e in ENGS}
        self.serial = {}
        self.last_tok = {e: [] for e in ENGS}
        self._hazard = {e: False for e in ENGS}
        for name in self.COMPUTE_SEMS:
            self._add_sem(stack, name)
        for name, mode in dma_streams:
            self._add_sem(stack, name)
            self.serial[name] = (mode == "serial")

    def _add_sem(self, stack, name):
        self.sem[name] = stack.enter_context(self.nc.semaphore(name))
        self.cnt[name] = 0
        for e in ENGS:
            self.seen[e][name] = 0

    SELF_SER = {"vector": "vec", "scalar": "act"}
    HAZARD_METHS = ("tensor_reduce", "reduce_max", "reduce_sum", "memset",
                    "reciprocal")

    def op(self, eng, meth, *args, wait=(), inc=None, **kw):
        wait = list(wait)
        ss = self.SELF_SER.get(eng)
        if ss is not None and meth != "_waits_only":
            if self._hazard.get(eng):
                wait.append((ss, self.cnt[ss]))
                self._hazard[eng] = False
            if meth in self.HAZARD_METHS:
                if inc is None:
                    inc = (ss, 1)
                self._hazard[eng] = True
        waits = []
        for w in wait:
            if w is None:
                continue
            s, v = w
            if v > self.seen[eng][s]:
                self.seen[eng][s] = v
                waits.append((s, v))
        tok = None
        if inc is not None:
            s, d = inc
            self.cnt[s] += d
            tok = (s, self.cnt[s])
            self.last_tok[eng] = [tok]
        self.rec[eng].append((meth, args, kw, tuple(waits),
                              (inc[0], inc[1]) if inc else None))
        return tok

    def dma(self, eng, stream, out, in_, wait=(), **kw):
        w = list(wait)
        if self.serial[stream]:
            w.append((stream, self.cnt[stream]))
        return self.op(eng, "dma_start", out=out, in_=in_, wait=w,
                       inc=(stream, 16), **kw)

    def mm(self, out, lhsT, rhs, start, stop, wait=(), inc=None):
        return self.op("tensor", "matmul", out, lhsT, rhs, start=start, stop=stop,
                       wait=wait, inc=inc)

    def barrier(self):
        toks = []
        for e in ENGS:
            toks += self.last_tok[e]
        for e in ENGS:
            self.op(e, "_waits_only", wait=toks)
        return toks

    def check(self):
        final = dict(self.cnt)
        inc_by = {}
        for e in ENGS:
            for meth, args, kw, waits, inc in self.rec[e]:
                if inc:
                    inc_by.setdefault(inc[0], {}).setdefault(e, 0)
                    inc_by[inc[0]][e] += inc[1]
        for e in ENGS:
            own = {}
            for i, (meth, args, kw, waits, inc) in enumerate(self.rec[e]):
                for (s, v) in (waits if meth != "_waits_only" else waits):
                    assert v <= final.get(s, 0), \
                        f"unreachable wait {s}>={v} (final {final.get(s)}) on {e} op {i} {meth}"
                    others = final.get(s, 0) - inc_by.get(s, {}).get(e, 0)
                    need_own = v - others
                    assert need_own <= own.get(s, 0), \
                        f"self-starvation: {e} op {i} {meth} waits {s}>={v}; " \
                        f"needs {need_own} own incs but only {own.get(s, 0)} so far"
                if meth == "_waits_only":
                    for (s, v) in args:
                        pass
                if inc:
                    own[inc[0]] = own.get(inc[0], 0) + inc[1]

    def replay(self, block):
        self.check()
        sems = self.sem
        for eng_name in ENGS:
            ops = self.rec[eng_name]

            def make(ops=ops):
                def body(e):
                    for meth, args, kw, waits, inc in ops:
                        for (s, v) in waits:
                            e.wait_ge(sems[s], v)
                        if meth == "_waits_only":
                            continue
                        r = getattr(e, meth)(*args, **kw)
                        if inc is not None:
                            r.then_inc(sems[inc[0]], inc[1])
                return body

            getattr(block, eng_name)(make())



def chunk_pf(w):
    n, m = w.shape
    assert n % 128 == 0, (n, m)
    return np.ascontiguousarray(
        w.reshape(n // 128, 128, m).transpose(1, 0, 2).reshape(128, -1))


def make_consts():
    half = DR // 2
    theta = 1.0 / (10000.0 ** (2.0 * np.arange(half, dtype=np.float64) / DR))
    ang = np.arange(S, dtype=np.float64)[None, :] * theta[:, None]
    cosA = np.cos(ang).astype(np.float32)
    sinA = np.sin(ang).astype(np.float32)
    k = np.arange(128)
    maskT = np.where(k[:, None] > k[None, :], np.float32(NEGBIG),
                     np.float32(0.0))
    ident = np.eye(128, dtype=np.float32)
    identb = np.eye(128, dtype=np.float32)
    ones2d = np.ones((128, 128), dtype=np.float32)
    basis = np.zeros((8, 1024), dtype=np.float32)
    for e in range(8):
        basis[e, e * 128:(e + 1) * 128] = 1.0
    return cosA, sinA, maskT, ident, identb, ones2d, basis


class Ctx:
    pass



def build_ln(c, x_sb, x_ready, banks, with_gate):
    P = c.P
    ones_col = c.ones2d_sb[:, 0:1]
    sq = [c.ln_scratch2[0][:, 0:512], c.ln_scratch2[1][:, 0:512]]
    sq_tok = [None, None]
    mm_after_sq = [None, None]
    stat_done = []
    for kc in range(NKC):
        for th in range(2):
            idx = (kc * 2 + th) % 2
            xs = x_sb[:, kc * T + th * 512: kc * T + (th + 1) * 512]
            sq_tok[idx] = P.op("scalar", "activation", sq[idx], xs, AF.Square,
                               wait=[x_ready, mm_after_sq[idx]], inc=("act", 1))
            P.mm(banks[th][0:1, :], ones_col, xs, start=(kc == 0), stop=(kc == NKC - 1),
                 wait=[x_ready, c.bank_guard.get(th)])
            tok = P.mm(banks[2 + th][0:1, :], ones_col, sq[idx],
                       start=(kc == 0), stop=(kc == NKC - 1),
                       wait=[sq_tok[idx], c.bank_guard.get(2 + th)], inc=("pe", 1))
            mm_after_sq[idx] = tok
            if kc == NKC - 1:
                stat_done.append(tok)
    r0, r1, r2, r3 = c.stat_rows
    sums, sumsq, mean, negmean = r0, r1, r2, r3
    c.ln_stat_done = stat_done[-1]
    ev = []
    for th in range(2):
        sl = slice(th * 512, (th + 1) * 512)
        ev.append(P.op("vector", "tensor_copy", sums[0:1, sl], banks[th][0:1, :],
                       wait=stat_done, inc=("vec", 1)))
        ev.append(P.op("vector", "tensor_copy", sumsq[0:1, sl], banks[2 + th][0:1, :],
                       wait=stat_done, inc=("vec", 1)))
    t0 = P.op("vector", "tensor_scalar", mean[0:1, :], sums[0:1, :], 1.0 / D, None,
              ALU.mult, wait=ev, inc=("vec", 1))
    t1 = P.op("vector", "tensor_scalar", negmean[0:1, :], sums[0:1, :], -1.0 / D, None,
              ALU.mult, wait=ev, inc=("vec", 1))
    t2 = P.op("vector", "tensor_scalar", r0[0:1, :], sumsq[0:1, :], 1.0 / D, None,
              ALU.mult, wait=ev + [t0, t1], inc=("vec", 1))
    t3 = P.op("vector", "tensor_tensor", r1[0:1, :], mean[0:1, :], mean[0:1, :],
              ALU.mult, wait=[t0, t2], inc=("vec", 1))
    t4 = P.op("vector", "tensor_tensor", r0[0:1, :], r0[0:1, :], r1[0:1, :],
              ALU.subtract, wait=[t3], inc=("vec", 1))
    t5 = P.op("scalar", "activation", r1[0:1, :], r0[0:1, :], AF.Ln,
              bias=c.eps_sb[0:1, 0:1], wait=[t4, c.eps_tok], inc=("act", 1))
    t6 = P.op("scalar", "activation", r2[0:1, :], r1[0:1, :], AF.Exp, scale=-0.5,
              wait=[t5], inc=("act", 1))
    rstd = r2
    ones_row = c.ones2d_sb[0:1, 0:128]
    btoks = []
    prev = None
    for th in range(2):
        sl = slice(th * 512, (th + 1) * 512)
        tokm = P.mm(banks[0][:, :], ones_row, negmean[0:1, sl], start=True, stop=True,
                    wait=[t1] + ev + ([prev] if prev else []), inc=("pe", 1))
        cm = P.op("vector", "tensor_copy", c.nm_b[:, sl], banks[0][:, :],
                  wait=[tokm], inc=("vec", 1))
        tokr = P.mm(banks[1][:, :], ones_row, rstd[0:1, sl], start=True, stop=True,
                    wait=[t6] + ev + ([prev] if prev else []), inc=("pe", 1))
        cr = P.op("vector", "tensor_copy", c.rs_b[:, sl], banks[1][:, :],
                  wait=[tokr], inc=("vec", 1))
        btoks += [cm, cr]
        prev = cr
    bc_done = btoks[-1]

    lt = c.ln_scratch2
    lnf = lt
    xn_toks = {}
    lnf_consumed = [None, None]
    gate_last = [None] * 8
    for kc in range(NKC):
        idx = kc % 2
        sl = slice(kc * T, kc * T + 1024)
        a = P.op("vector", "tensor_tensor", lt[idx][:, :], x_sb[:, sl], c.nm_b[:, 0:1024],
                 ALU.add,
                 wait=[bc_done, x_ready, stat_done[-1]] + (lnf_consumed[idx] or []),
                 inc=("vec", 1))
        b = P.op("vector", "tensor_tensor", lnf[idx][:, :], lt[idx][:, :],
                 c.rs_b[:, 0:1024], ALU.mult, wait=[a], inc=("vec", 1))
        xt = P.op("scalar", "activation", c.xn_sb[:, sl], lnf[idx][:, :], AF.Copy,
                  wait=[b], inc=("act", 1))
        xn_toks[kc] = xt
        if with_gate:
            for tc in range(8):
                w0 = [b] + (btoks + ev + [c.bank_guard.get(tc)] if kc == 0 else [])
                gate_last[tc] = P.mm(banks[tc][0:128, 0:8],
                                     lnf[idx][:, tc * 128:(tc + 1) * 128],
                                     c.wg_sb[:, kc * 8:(kc + 1) * 8],
                                     start=(kc == 0), stop=(kc == NKC - 1), wait=w0,
                                     inc=("pe", 1))
            lnf_consumed[idx] = [gate_last[7], xt]
        else:
            lnf_consumed[idx] = [xt]

    if not with_gate:
        return xn_toks, None

    gT_toks = []
    prev_p = None
    for tc in range(8):
        l8 = c.gate_sb[:, tc * 8:(tc + 1) * 8]
        ev2 = P.op("vector", "tensor_copy", l8, banks[tc][0:128, 0:8],
                   wait=[gate_last[tc]], inc=("vec", 1))
        col = c.gcol_sb
        m1 = P.op("vector", "reduce_max", col[:, 0:1], l8, axis=AX, wait=[ev2],
                  inc=("vec", 1))
        eq = P.op("vector", "tensor_scalar", c.geq_sb[:, 0:8], l8, col[:, 0:1], None,
                  ALU.is_equal, wait=[m1], inc=("vec", 1))
        l2 = P.op("vector", "scalar_tensor_tensor", c.geq_sb[:, 8:16],
                  c.geq_sb[:, 0:8], NEGBIG, l8, ALU.mult, ALU.add,
                  wait=[eq, prev_p], inc=("vec", 1))
        m2 = P.op("vector", "reduce_max", col[:, 1:2], c.geq_sb[:, 8:16], axis=AX,
                  wait=[l2], inc=("vec", 1))
        ge = P.op("vector", "tensor_scalar", c.geq_sb[:, 0:8], l8, col[:, 1:2], None,
                  ALU.is_ge, wait=[m2], inc=("vec", 1))
        nm1 = P.op("vector", "tensor_scalar", col[:, 2:3], col[:, 0:1], -1.0, None,
                   ALU.mult, wait=[m1, prev_p], inc=("vec", 1))
        p = P.op("scalar", "activation", c.geq_sb[:, 16:24], l8, AF.Exp,
                 bias=col[:, 2:3], wait=[nm1, ge], inc=("act", 1))
        prev_p = p
        pm = P.op("vector", "tensor_tensor", c.geq_sb[:, 24:32], c.geq_sb[:, 16:24],
                  c.geq_sb[:, 0:8], ALU.mult, wait=[p], inc=("vec", 1))
        sm = P.op("vector", "reduce_sum", col[:, 3:4], c.geq_sb[:, 24:32], axis=AX,
                  wait=[pm], inc=("vec", 1))
        rs = P.op("vector", "reciprocal", col[:, 4:5], col[:, 3:4], wait=[sm],
                  inc=("vec", 1))
        g = P.op("vector", "tensor_scalar", c.gres_sb[:, tc * 8:(tc + 1) * 8],
                 c.geq_sb[:, 24:32], col[:, 4:5], None, ALU.mult, wait=[rs],
                 inc=("vec", 1))
        tk = P.op("tensor", "transpose", banks[tc][0:8, 0:128],
                  c.gres_sb[:, tc * 8:(tc + 1) * 8], c.ident_sb[:, :],
                  wait=[g], inc=("pe", 1))
        gT_toks.append(P.op("vector", "tensor_copy",
                            c.gT_sb[0:8, tc * 128:(tc + 1) * 128],
                            banks[tc][0:8, 0:128], wait=[tk], inc=("vec", 1)))
    return xn_toks, gT_toks



def build_ffn(c, x_sb, x_ready, banks):
    P = c.P
    xn_toks, gT_toks = build_ln(c, x_sb, x_ready, banks, with_gate=True)

    ring = c.ring
    ring_free = c.ring_free
    ring_i = c.ring_i

    def load_piece(dram_ap, extra=()):
        i = ring_i[0] % 4
        ring_i[0] += 1
        tok = P.dma("gpsimd", f"r{i}", ring[i][:, :], dram_ap,
                    wait=[ring_free[i]] + list(extra))
        return i, tok

    bank_free = {i: gT_toks[i] for i in range(8)}
    gb_tok = [None, None]
    st_prev = {}

    for e in range(E):
        gbi = e % 2
        gb_ready = []
        for th in range(2):
            tok = P.mm(banks[6][:, :], c.basis_sb[0:8, e * 128:(e + 1) * 128],
                       c.gT_sb[0:8, th * 512:(th + 1) * 512], start=True, stop=True,
                       wait=gT_toks + [gb_tok[gbi], bank_free[6]], inc=("pe", 1))
            cpy = P.op("vector", "tensor_copy", c.gb_sb[gbi][:, th * 512:(th + 1) * 512],
                       banks[6][:, :], wait=[tok], inc=("vec", 1))
            bank_free[6] = cpy
            gb_ready.append(cpy)
        gb_tok[gbi] = gb_ready[-1]

        p_wi = {}
        p_wi[(0, 0)] = load_piece(c.wi0p[e, 0])
        p_wi[(1, 0)] = load_piece(c.wi1p[e, 0])
        p_wi[(0, 1)] = load_piece(c.wi0p[e, 1])
        p_wi[(1, 1)] = load_piece(c.wi1p[e, 1])

        inter_toks = []
        wo_pieces = []
        for mq in range(4):
            s0, t0 = p_wi[(0, mq)]
            s1, t1 = p_wi[(1, mq)]
            for mi in range(2):
                m = mq * 2 + mi
                pA = [banks[0], banks[1]] if mi == 0 else [banks[2], banks[3]]
                pB = [banks[2], banks[3]] if mi == 0 else [banks[0], banks[1]]
                for kc in range(NKC):
                    lhs = ring[s0][:, kc * 256 + mi * 128: kc * 256 + (mi + 1) * 128]
                    P.mm(pA[0][:, :], lhs,
                         c.xn_sb[:, kc * T: kc * T + 512],
                         start=(kc == 0), stop=(kc == NKC - 1),
                         wait=[t0, xn_toks[kc], bank_free[0 if mi == 0 else 2]])
                    h0 = P.mm(pA[1][:, :], lhs,
                              c.xn_sb[:, kc * T + 512: kc * T + 1024],
                              start=(kc == 0), stop=(kc == NKC - 1),
                              wait=[bank_free[1 if mi == 0 else 3]],
                              inc=("pe", 1) if kc == NKC - 1 else None)
                for kc in range(NKC):
                    lhs = ring[s1][:, kc * 256 + mi * 128: kc * 256 + (mi + 1) * 128]
                    P.mm(pB[0][:, :], lhs,
                         c.xn_sb[:, kc * T: kc * T + 512],
                         start=(kc == 0), stop=(kc == NKC - 1),
                         wait=[t1, bank_free[2 if mi == 0 else 0]])
                    h1 = P.mm(pB[1][:, :], lhs,
                              c.xn_sb[:, kc * T + 512: kc * T + 1024],
                              start=(kc == 0), stop=(kc == NKC - 1),
                              wait=[bank_free[3 if mi == 0 else 1]],
                              inc=("pe", 1) if kc == NKC - 1 else None)
                for th in range(2):
                    iA = (0 if mi == 0 else 2) + th
                    iB = (2 if mi == 0 else 0) + th
                    sidx = th
                    sv = P.op("scalar", "activation", c.s_buf[sidx][:, :],
                              pA[th][:, :], AF.Silu, wait=[h0, c.s_free[sidx]],
                              inc=("act", 1))
                    bank_free[iA] = sv
                    hg = P.op("vector", "tensor_tensor", c.hg_buf[sidx][:, :],
                              pB[th][:, :],
                              c.gb_sb[gbi][:, th * 512:(th + 1) * 512], ALU.mult,
                              wait=[h1, c.hg_free[sidx]] + gb_ready, inc=("vec", 1))
                    bank_free[iB] = hg
                    it = P.op("vector", "tensor_tensor",
                              c.inter_sb[:, m * 1024 + th * 512: m * 1024 + (th + 1) * 512],
                              c.s_buf[sidx][:, :], c.hg_buf[sidx][:, :], ALU.mult,
                              wait=[sv, hg], inc=("vec", 1))
                    c.s_free[sidx] = it
                    c.hg_free[sidx] = it
                    inter_toks.append(it)
            pe_now = ("pe", P.cnt["pe"])
            ring_free[s0] = pe_now
            ring_free[s1] = pe_now
            if mq == 1:
                p_wi[(0, 2)] = load_piece(c.wi0p[e, 2])
                p_wi[(1, 2)] = load_piece(c.wi1p[e, 2])
            elif mq == 2:
                p_wi[(0, 3)] = load_piece(c.wi0p[e, 3])
                p_wi[(1, 3)] = load_piece(c.wi1p[e, 3])
            elif mq == 3:
                wo_pieces.append(load_piece(c.wop[e, 0]))
                wo_pieces.append(load_piece(c.wop[e, 1]))

        inter_done = inter_toks[-1]
        for dq in range(4):
            sw, tw = wo_pieces[dq]
            for dm in range(4):
                dmg = dq * 4 + dm
                pw = [banks[4], banks[5]] if dm % 2 == 0 else [banks[6], banks[7]]
                bi = 4 if dm % 2 == 0 else 6
                for kc in range(8):
                    lhs = ring[sw][:, kc * 512 + dm * 128: kc * 512 + (dm + 1) * 128]
                    P.mm(pw[0][:, :], lhs,
                         c.inter_sb[:, kc * 1024: kc * 1024 + 512],
                         start=(kc == 0), stop=(kc == 7),
                         wait=[tw, inter_done, bank_free[bi]])
                    wo = P.mm(pw[1][:, :], lhs,
                              c.inter_sb[:, kc * 1024 + 512: kc * 1024 + 1024],
                              start=(kc == 0), stop=(kc == 7),
                              wait=[bank_free[bi + 1]],
                              inc=("pe", 1) if kc == 7 else None)
                for th in range(2):
                    sl = slice(dmg * T + th * 512, dmg * T + (th + 1) * 512)
                    stt = P.op("vector", "scalar_tensor_tensor", x_sb[:, sl],
                               pw[th][:, :], 0.5, x_sb[:, sl], ALU.mult, ALU.add,
                               wait=[wo, st_prev.get((dmg, th))], inc=("vec", 1))
                    st_prev[(dmg, th)] = stt
                    bank_free[bi + th] = stt
            ring_free[sw] = ("pe", P.cnt["pe"])
            if dq == 0:
                wo_pieces.append(load_piece(c.wop[e, 2]))
            elif dq == 1:
                wo_pieces.append(load_piece(c.wop[e, 3]))
    c.bank_guard = dict(bank_free)
    return [st_prev[(g2, th)] for g2 in range(16) for th in range(2)]





def build_delta_trig(c):
    P = c.P
    d = c.dt_sb
    di = c.dti_sb
    ld = P.dma("sync", "ddl", d[0:32, 0:1], c.delta)
    t0 = P.op("vector", "tensor_scalar", d[0:32, 1:2], d[0:32, 0:1],
              1.0 / TWO_PI, None, ALU.mult, wait=[ld], inc=("vec", 1))
    t1 = P.op("vector", "tensor_copy", di[0:32, 0:1], d[0:32, 1:2], wait=[t0],
              inc=("vec", 1))
    t2 = P.op("vector", "tensor_copy", d[0:32, 2:3], di[0:32, 0:1], wait=[t1],
              inc=("vec", 1))
    t3 = P.op("vector", "scalar_tensor_tensor", d[0:32, 3:4], d[0:32, 2:3],
              -TWO_PI, d[0:32, 0:1], ALU.mult, ALU.add, wait=[t2], inc=("vec", 1))
    s0 = P.op("scalar", "activation", d[0:32, 4:5], d[0:32, 3:4], AF.Sin,
              wait=[t3], inc=("act", 1))
    s1 = P.op("scalar", "activation", d[0:32, 5:6], d[0:32, 3:4], AF.Sin,
              scale=0.5, wait=[t3], inc=("act", 1))
    s2 = P.op("scalar", "activation", d[0:32, 6:7], d[0:32, 5:6], AF.Square,
              wait=[s1, t3], inc=("act", 1))
    t4 = P.op("vector", "tensor_scalar", d[0:32, 7:8], d[0:32, 6:7], -2.0, 1.0,
              ALU.mult, ALU.add, wait=[s2], inc=("vec", 1))
    c.sin_d = d[0:32, 4:5]
    c.cos_d = d[0:32, 7:8]
    return [s0, t4]


def build_angle_tables(c, banks):
    P = c.P
    sc = c.pope_scr
    t1 = sc[0:32, 0:512]
    t2 = sc[0:32, 512:1024]
    t3 = sc[0:32, 1024:1536]
    t4 = sc[0:32, 1536:2048]
    toks = []
    prev = None
    for tb in range(16):
        sl = slice(tb * 512, (tb + 1) * 512)
        la = P.dma("gpsimd", "ang0", c.angb[0][0:32, 0:512], c.cosA[0:32, sl],
                   wait=[prev])
        lb = P.dma("gpsimd", "ang0", c.angb[0][0:32, 512:1024], c.sinA[0:32, sl])
        w0 = [la, lb, prev] + c.trig_toks
        a0 = P.op("vector", "tensor_scalar", t1, c.angb[0][0:32, 0:512], c.cos_d,
                  None, ALU.mult, wait=w0, inc=("vec", 1))
        a1 = P.op("vector", "tensor_scalar", t2, c.angb[0][0:32, 512:1024],
                  c.sin_d, None, ALU.mult, inc=("vec", 1))
        a2 = P.op("vector", "tensor_tensor", c.cosAng[0:32, sl], t1, t2,
                  ALU.subtract, wait=[a0, a1], inc=("vec", 1))
        a3 = P.op("vector", "tensor_scalar", t3, c.angb[0][0:32, 0:512], c.sin_d,
                  None, ALU.mult, inc=("vec", 1))
        a4 = P.op("vector", "tensor_scalar", t4, c.angb[0][0:32, 512:1024],
                  c.cos_d, None, ALU.mult, inc=("vec", 1))
        a5 = P.op("vector", "tensor_tensor", c.sinAng[0:32, sl], t3, t4,
                  ALU.add, wait=[a3, a4], inc=("vec", 1))
        prev = a5
        toks.append(a5)
    c.ang_done = toks[-1]
    return toks


def build_pope_chunk(c, raw1, raw2, pack, tb, wait):
    P = c.P
    pi = c.pope_i[0] % 2
    c.pope_i[0] += 1
    po = pi * 2048
    sl_tok = slice(tb * 512, (tb + 1) * 512)
    sc = c.pope_scr
    t1 = sc[0:32, pi * 1024: pi * 1024 + 512]
    t2 = sc[0:32, pi * 1024 + 512: pi * 1024 + 1024]
    sb = c.pope_bf
    mu1 = sb[0:32, po: po + 512]
    mu2 = sb[0:32, po + 512: po + 1024]
    rt1 = sb[0:32, po + 1024: po + 1536]
    rt2 = sb[0:32, po + 1536: po + 2048]
    o2 = c.rp_stage[0:32, pi * 512:(pi + 1) * 512]
    mus = []
    for i_, (raw, mu) in enumerate(((raw1, mu1), (raw2, mu2))):
        tr = rt1 if i_ == 0 else rt2
        b0 = P.op("scalar", "activation", t1, raw, AF.Abs,
                  wait=list(wait) + mus + [c.pope_free[pi]], inc=("act", 1))
        b1 = P.op("scalar", "activation", t2, t1, AF.Exp, scale=-1.0,
                  wait=[b0], inc=("act", 1))
        b2 = P.op("scalar", "activation", t1, t2, AF.Ln, bias=1.0,
                  wait=[b1], inc=("act", 1))
        b3 = P.op("vector", "tensor_scalar", tr, raw, 0.0, None, ALU.max,
                  wait=list(wait) + [c.pope_free[pi]], inc=("vec", 1))
        mus.append(P.op("vector", "tensor_tensor", mu, t1, tr, ALU.add,
                        wait=[b2, b3], inc=("vec", 1)))
    w0 = mus + [c.ang_done]
    r0 = P.op("vector", "tensor_tensor", rt1, mu1, c.cosAng[0:32, sl_tok],
              ALU.mult, wait=w0, inc=("vec", 1))
    r1 = P.op("vector", "tensor_tensor", rt2, mu2, c.sinAng[0:32, sl_tok],
              ALU.mult, wait=[r0], inc=("vec", 1))
    x1 = P.op("vector", "tensor_tensor", pack[0:32, sl_tok], rt1, rt2,
              ALU.subtract, wait=[r0, r1, c.pack_ms], inc=("vec", 1))
    r2 = P.op("vector", "tensor_tensor", rt1, mu1, c.sinAng[0:32, sl_tok],
              ALU.mult, wait=[x1], inc=("vec", 1))
    r3 = P.op("vector", "tensor_tensor", rt2, mu2, c.cosAng[0:32, sl_tok],
              ALU.mult, wait=[r2], inc=("vec", 1))
    x2 = P.op("vector", "tensor_tensor", o2, rt1, rt2, ALU.add,
              wait=[r2, r3, c.rp_free[pi]], inc=("vec", 1))
    dm = P.dma("scalar", f"rp{pi}", pack[32:64, sl_tok], o2,
               wait=[x2, c.pack_ms])
    c.rp_free[pi] = dm
    c.pope_free[pi] = x2
    return dm


def build_proj_only(c, xn_toks, banks):
    P = c.P
    wtok = c.wat_tok
    bank_free = {}

    def proj(out_sb, w_sb, nm):
        toks = {}
        for m in range(nm):
            for th in range(2):
                par = (m * 2 + th) % 4
                bk = banks[par]
                for kc in range(NKC):
                    mm = P.mm(bk[:, :],
                              w_sb[:, kc * (nm * 128) + m * 128:
                                   kc * (nm * 128) + (m + 1) * 128],
                              c.xn_sb[:, kc * T + th * 512: kc * T + (th + 1) * 512],
                              start=(kc == 0), stop=(kc == NKC - 1),
                              wait=[wtok, xn_toks[kc], bank_free.get(par)],
                              inc=("pe", 1) if kc == NKC - 1 else None)
                ev = P.op("scalar", "activation",
                          out_sb[:, m * T + th * 512: m * T + (th + 1) * 512],
                          bk[:, :], AF.Copy, wait=[mm], inc=("act", 1))
                bank_free[par] = ev
                toks[(m, th)] = ev
        return toks

    ckv_toks = proj(c.ckv_sb, c.wdkv_sb, 4)
    kr_toks = []
    for th in range(2):
        bk = banks[4 + th]
        for kc in range(NKC):
            mm = P.mm(bk[0:64, :],
                      c.wkr_sb[:, kc * 64: (kc + 1) * 64],
                      c.xn_sb[:, kc * T + th * 512: kc * T + (th + 1) * 512],
                      start=(kc == 0), stop=(kc == NKC - 1),
                      wait=[wtok, xn_toks[kc]],
                      inc=("pe", 1) if kc == NKC - 1 else None)
        kr_toks.append(P.op("scalar", "activation",
                            c.kr_sb[0:64, th * 512:(th + 1) * 512], bk[0:64, :],
                            AF.Copy, wait=[mm], inc=("act", 1)))
    wa = P.dma("sync", "agw1",
               c.ag1i[1024:1536].rearrange("(c p) t -> p c t", p=128),
               c.ckv_sb.ap().rearrange("p (c t) -> p c t", c=4),
               wait=list(ckv_toks.values()))
    wk = P.dma("sync", "agw1", c.ag1i[1536:1600], c.kr_sb[0:64, :],
               wait=kr_toks)
    cq_toks = proj(c.cq_sb, c.wdq_sb, 8)
    wq = P.dma("sync", "agw1",
               c.ag1i[0:1024].rearrange("(c p) t -> p c t", p=128),
               c.cq_sb.ap().rearrange("p (c t) -> p c t", c=8),
               wait=list(cq_toks.values()))
    return (wa, wk, wq)


def build_kr_tb(c, banks, tb, prevs):
    P = c.P
    g = tb // 2
    toff = (tb % 2) * 512
    kb = (tb % 2) * 1024
    lk1 = P.dma("sync", f"krs{tb % 2}", c.krb0[0:32, kb:kb + 512],
                c.ag1o[g * 1600 + 1536: g * 1600 + 1568, toff:toff + 512],
                wait=[c.cc2_tok, prevs[tb % 2]])
    lk2 = P.dma("sync", f"krs{tb % 2}", c.krb0[0:32, kb + 512:kb + 1024],
                c.ag1o[g * 1600 + 1568: g * 1600 + 1600, toff:toff + 512])
    tok = build_pope_chunk(c, c.krb0[0:32, kb:kb + 512],
                           c.krb0[0:32, kb + 512:kb + 1024],
                           c.krp, tb, [lk1, lk2])
    prevs[tb % 2] = tok
    c.kr_toks[tb] = tok


def _emit_pf(c, ptb):
    P = c.P
    ppb = ptb % 2
    pg = ptb // 2
    ptoff = (ptb % 2) * 512
    if ptb < 2:
        ckdst = c.ckb[ppb][:, :]
        cqdst = c.cqb[ppb][:, :]
    else:
        ckdst = c.ckb3[:, (ptb - 2) * 2048:(ptb - 1) * 2048]
        cqdst = c.cqb3[:, (ptb - 2) * 4096:(ptb - 1) * 4096]
    lk = P.dma("sync", f"pfk{ppb}", ckdst,
               c.ag1o[pg * 1600 + 1024: pg * 1600 + 1536]
               .rearrange("(c p) t -> p c t", p=128)[:, :, ptoff:ptoff + 512],
               wait=[c.cc1_tok, c.h_build_done])
    lq = P.dma("scalar", f"pfq{ppb}", cqdst,
               c.ag1o[pg * 1600: pg * 1600 + 1024]
               .rearrange("(c p) t -> p c t", p=128)[:, :, ptoff:ptoff + 512],
               wait=[c.cc1_tok, c.h_build_done])
    c.pf_toks[ptb] = (lk, lq)


def build_head(c, hh, banks, head_guard):
    P = c.P
    wat2 = c.wat2_tok
    kc_tok = {}
    v_tok = {}
    q_tok = {}
    qr_tok = {}
    if hh == 0:
        c.vmemset = P.op("vector", "memset", c.vaug_sb[:, :], 1.0, inc=("vec", 1))
    c.qn_toks = []
    c.kn_toks = []
    pend_pope = None
    prev_build = c.h_build_done if hh == 1 else None
    sbk = [prev_build, prev_build]
    for tb in range(16):
        pb = tb % 2
        g = tb // 2
        toff = (tb % 2) * 512
        ckS, cko = c.ckb[pb], 0
        cqS, cqo = c.cqb[pb], 0
        if hh == 1 and tb < 4:
            lck, lcq = c.pf_toks[tb]
            if tb >= 2:
                ckS, cko = c.ckb3, (tb - 2) * 2048
                cqS, cqo = c.cqb3, (tb - 2) * 4096
        else:
            lck = P.dma("gpsimd", f"ck{pb}", c.ckb[pb][:, :],
                        c.ag1o[g * 1600 + 1024: g * 1600 + 1536]
                        .rearrange("(c p) t -> p c t", p=128)[:, :, toff:toff + 512],
                        wait=[c.cc1_tok, sbk[pb]])
            lcq = P.dma("gpsimd", f"cq{pb}", c.cqb[pb][:, :],
                        c.ag1o[g * 1600: g * 1600 + 1024]
                        .rearrange("(c p) t -> p c t", p=128)[:, :, toff:toff + 512],
                        wait=[c.cc1_tok, sbk[pb]])
        bk = banks[pb]
        for kc in range(4):
            mm = P.mm(bk[:, :], c.wuk_sb[hh][:, kc * 128:(kc + 1) * 128],
                      ckS[:, cko + kc * 512: cko + (kc + 1) * 512],
                      start=(kc == 0), stop=(kc == 3),
                      wait=[wat2, lck, c.kcfree[pb], head_guard],
                      inc=("pe", 1) if kc == 3 else None)
        kc_tok[tb] = P.op("vector", "tensor_copy",
                          c.kcbuf[:, tb * 512:(tb + 1) * 512], bk[:, :],
                          wait=[mm], inc=("vec", 1))
        c.kcfree[pb] = kc_tok[tb]
        bv = banks[2 + pb]
        for ts in range(4):
            j = tb * 4 + ts
            for kc in range(4):
                mm = P.mm(bv[:, ts * 128:(ts + 1) * 128],
                          ckS[:, cko + kc * 512 + ts * 128:
                              cko + kc * 512 + (ts + 1) * 128],
                          c.wuv_sb[hh][:, kc * 128:(kc + 1) * 128],
                          start=(ts == 0 and kc == 0), stop=(kc == 3),
                          wait=[wat2, lck, c.vfree[pb], head_guard, c.vmemset],
                          inc=("pe", 1) if (ts == 3 and kc == 3) else None)
        vev = P.op("vector", "tensor_copy",
                   c.vaug_sb.ap()
                   .rearrange("p (j d) -> p j d", d=129)[:, tb * 4:(tb + 1) * 4, 0:128],
                   bv[:, 0:512].rearrange("p (j d) -> p j d", d=128),
                   wait=[mm], inc=("vec", 1))
        for ts in range(4):
            v_tok[tb * 4 + ts] = vev
        c.vfree[pb] = vev
        bq = banks[4 + pb]
        for kc in range(8):
            mm = P.mm(bq[:, :], c.wuq_sb[hh][:, kc * 128:(kc + 1) * 128],
                      cqS[:, cqo + kc * 512: cqo + (kc + 1) * 512],
                      start=(kc == 0), stop=(kc == 7),
                      wait=[wat2, lcq, c.qcfree[pb], head_guard],
                      inc=("pe", 1) if kc == 7 else None)
        q_tok[tb] = P.op("vector", "tensor_copy",
                         c.qcbuf[:, tb * 512:(tb + 1) * 512], bq[:, :],
                         wait=[mm], inc=("vec", 1))
        c.qcfree[pb] = q_tok[tb]
        bg = banks[6]
        bq2 = banks[4 + (1 - pb)]
        for kc in range(8):
            mm1 = P.mm(bg[0:32, :], c.wqr_sb[hh][:, kc * 64: kc * 64 + 32],
                       cqS[:, cqo + kc * 512: cqo + (kc + 1) * 512],
                       start=(kc == 0), stop=(kc == 7),
                       wait=[wat2, lcq, c.gfree, head_guard],
                       inc=("pe", 1) if kc == 7 else None)
        for kc in range(8):
            mm2 = P.mm(bq2[0:32, :], c.wqr_sb[hh][:, kc * 64 + 32: kc * 64 + 64],
                       cqS[:, cqo + kc * 512: cqo + (kc + 1) * 512],
                       start=(kc == 0), stop=(kc == 7),
                       wait=[wat2, lcq, c.qcfree[1 - pb], head_guard],
                       inc=("pe", 1) if kc == 7 else None)
        qb_ = pb * 1024
        rw1 = P.op("vector", "tensor_copy", c.qraw[0:32, qb_:qb_ + 512],
                   bg[0:32, :], wait=[mm1, c.qrawfree[pb]], inc=("vec", 1))
        c.gfree = rw1
        rw2 = P.op("vector", "tensor_copy", c.qraw[0:32, qb_ + 512:qb_ + 1024],
                   bq2[0:32, :], wait=[mm2], inc=("vec", 1))
        c.qcfree[1 - pb] = rw2
        if pend_pope is not None:
            ptb, prw1, prw2, ppb = pend_pope
            pqb = ppb * 1024
            qr_tok[ptb] = build_pope_chunk(c, c.qraw[0:32, pqb:pqb + 512],
                                           c.qraw[0:32, pqb + 512:pqb + 1024],
                                           c.qrp, ptb, [prw1, prw2])
            c.qrawfree[ppb] = qr_tok[ptb]
            if hh == 0:
                build_kr_tb(c, banks, ptb, c.kr_prevs)
        pend_pope = (tb, rw1, rw2, pb)
        sbk[pb] = ("pe", P.cnt["pe"])
    c.h_build_done = ("pe", P.cnt["pe"])
    if hh == 0:
        c.pf_toks = {}
        for ptb in range(2):
            _emit_pf(c, ptb)
    ptb, prw1, prw2, ppb = pend_pope
    pqb = ppb * 1024
    qr_tok[ptb] = build_pope_chunk(c, c.qraw[0:32, pqb:pqb + 512],
                                   c.qraw[0:32, pqb + 512:pqb + 1024],
                                   c.qrp, ptb, [prw1, prw2])
    c.qrawfree[ppb] = qr_tok[ptb]
    if hh == 0:
        build_kr_tb(c, banks, ptb, c.kr_prevs)
    build_done = [kc_tok[15], q_tok[15], qr_tok[15], v_tok[63]]

    sq_free = [None, None]
    bank_n = [banks[0], banks[6]]
    bank_n_free = [c.scfree[0], c.gfree]
    for i, tc in enumerate(range(16)):
        slq = slice(tc * 512, (tc + 1) * 512)
        pi2 = i % 2
        bn = bank_n[pi2]
        sq1 = P.op("scalar", "activation", c.sqn[:, 0:512], c.qcbuf[:, slq],
                   AF.Square, wait=[q_tok[tc], sq_free[0]], inc=("act", 1))
        sq2 = P.op("scalar", "activation", c.sqr[0:64, 0:512],
                   c.qrp[0:64, slq], AF.Square, wait=[qr_tok[tc], sq_free[1]],
                   inc=("act", 1))
        sq3 = P.op("scalar", "activation", c.sqk[:, 0:512], c.kcbuf[:, slq],
                   AF.Square, wait=[kc_tok[tc], sq_free[0], sq_free[1]],
                   inc=("act", 1))
        mm_ = P.mm(bn[0:1, 0:512], c.onesb[:, 0:1], c.sqn[:, 0:512],
                   start=True, stop=False, wait=[sq1, bank_n_free[pi2]])
        mm_ = P.mm(bn[0:1, 0:512], c.onesb[0:64, 0:1], c.sqr[0:64, 0:512],
                   start=False, stop=True, wait=[sq2], inc=("pe", 1))
        rm = P.op("vector", "reduce_max", c.mxp[0:1, tc: tc + 1],
                  bn[0:1, 0:512], axis=AX, wait=[mm_], inc=("vec", 1))
        c.qn_toks.append(rm)
        mmk = P.mm(bn[32:33, 0:512], c.onesb[:, 0:1], c.sqk[:, 0:512],
                   start=True, stop=True, wait=[sq3, rm], inc=("pe", 1))
        sq_free[0] = mmk
        rmk = P.op("vector", "reduce_max", c.mxp[0:1, 16 + tc: 17 + tc],
                   bn[32:33, 0:512], axis=AX, wait=[mmk], inc=("vec", 1))
        c.kn_toks.append(rmk)
        bank_n_free[pi2] = rmk
        sq_free[1] = mmk
        if hh == 0:
            sqv = P.op("scalar", "activation", c.sqk[0:64, 0:512],
                       c.krp[0:64, slq], AF.Square,
                       wait=[c.kr_toks[tc], rmk], inc=("act", 1))
            mmv = P.mm(bn[64:65, 0:512], c.onesb[0:64, 0:1],
                       c.sqk[0:64, 0:512], start=True, stop=True,
                       wait=[sqv, rmk], inc=("pe", 1))
            rmv = P.op("vector", "reduce_max", c.mxp[0:1, 32 + tc: 33 + tc],
                       bn[64:65, 0:512], axis=AX, wait=[mmv], inc=("vec", 1))
            bank_n_free[pi2] = rmv
            sq_free[0] = mmv
            c.kr_nmax.append(rmv)
    if hh == 0:
        c.krmax_tok = P.op("vector", "reduce_max", c.krmax[0:1, 0:1],
                           c.mxp[0:1, 32:48], axis=AX, wait=c.kr_nmax,
                           inc=("vec", 1))
    c.gfree = bank_n_free[1]
    c.scfree[0] = bank_n_free[0]

    m0 = P.op("vector", "reduce_max", c.mxq[0:1, 0:1], c.mxp[0:1, 0:16], axis=AX,
              wait=c.qn_toks, inc=("vec", 1))
    m1 = P.op("vector", "reduce_max", c.mxq[0:1, 1:2], c.mxp[0:1, 16:32], axis=AX,
              wait=c.kn_toks, inc=("vec", 1))
    m2 = P.op("vector", "tensor_tensor", c.mxq[0:1, 2:3], c.mxq[0:1, 1:2],
              c.krmax[0:1, 0:1], ALU.add, wait=[m1, c.krmax_tok], inc=("vec", 1))
    m3 = P.op("vector", "tensor_tensor", c.mxq[0:1, 3:4], c.mxq[0:1, 0:1],
              c.mxq[0:1, 2:3], ALU.mult, wait=[m0, m2], inc=("vec", 1))
    l0 = P.op("scalar", "activation", c.mxq[0:1, 4:5], c.mxq[0:1, 3:4], AF.Ln,
              wait=[m3], inc=("act", 1))
    l1 = P.op("scalar", "activation", c.mxq[0:1, 5:6], c.mxq[0:1, 4:5], AF.Exp,
              scale=0.5, bias=c.clog_sb[0:1, 0:1], wait=[l0, c.clog_tok],
              inc=("act", 1))
    l2 = P.op("vector", "tensor_scalar", c.mxq[0:1, 6:7], c.mxq[0:1, 5:6], -1.0,
              None, ALU.mult, wait=[l1], inc=("vec", 1))
    bmm = P.mm(banks[6][:, 0:1], c.ones2d_sb[0:1, 0:128], c.mxq[0:1, 6:7],
               start=True, stop=True, wait=[l2, c.gfree], inc=("pe", 1))
    negc = P.op("vector", "tensor_copy", c.negc_sb[:, 0:1], banks[6][:, 0:1],
                wait=[bmm], inc=("vec", 1))
    c.gfree = negc
    if hh == 0:
        for ptb in (2, 3):
            _emit_pf(c, ptb)

    probs_free = c.probs_free
    av_free = c.av_free
    scfree = c.scfree
    out_toks = []
    for s in range(16):
        avstart = {t: True for t in range(4)}
        pend = None

        def emit_av(jv, eev):
            nonlocal_last = None
            for t in range(max(0, jv - 4 * s), 4):
                ba = c.avbanks[t]
                tk = P.mm(ba[:, 0:129], c.probs[jv % 2][:, t * 128:(t + 1) * 128],
                          c.vaug_sb[:, jv * 129:(jv + 1) * 129],
                          start=avstart[t], stop=(jv == 4 * s + t),
                          wait=[eev, v_tok[jv], av_free[t]],
                          inc=("pe", 1) if t == 3 else None)
                avstart[t] = False
            return tk

        for j in range(4 * s + 4):
            pb = j % 2
            bsc = banks[pb]
            mm1 = P.mm(bsc[:, :], c.kcbuf[:, j * 128:(j + 1) * 128],
                       c.qcbuf[:, s * 512:(s + 1) * 512], start=True, stop=False,
                       wait=[kc_tok[j // 4], q_tok[s], scfree[pb], negc,
                             build_done[0]])
            mm2 = P.mm(bsc[:, :], c.krp[:, j * 128:(j + 1) * 128],
                       c.qrp[:, s * 512:(s + 1) * 512], start=False,
                       stop=True, wait=[c.kr_toks[j // 4], qr_tok[s]],
                       inc=("pe", 1))
            mtok = mm2
            if 0 <= j - 4 * s <= 3:
                t = j - 4 * s
                mtok = P.op("vector", "tensor_tensor",
                            bsc[:, t * 128:(t + 1) * 128],
                            bsc[:, t * 128:(t + 1) * 128], c.maskT_sb[:, :],
                            ALU.add, wait=[mm2], inc=("vec", 1))
            ee = P.op("scalar", "activation", c.probs[pb][:, :], bsc[:, :],
                      AF.Exp, scale=SCALE, bias=c.negc_sb[:, 0:1],
                      wait=[mtok, probs_free[pb], negc], inc=("act", 1))
            scfree[pb] = ee
            if pend is not None:
                probs_free[pend[0] % 2] = emit_av(*pend)
            pend = (j, ee)
        probs_free[pend[0] % 2] = emit_av(*pend)
        ot = []
        for t in range(4):
            ba = c.avbanks[t]
            dn = P.op("vector", "tensor_scalar", c.rcol[:, t: t + 1],
                      ba[:, 128:129], 1e-30, None, ALU.max,
                      wait=[probs_free[0], probs_free[1]], inc=("vec", 1))
            rc = P.op("vector", "reciprocal", c.rcol[:, 4 + t: 5 + t],
                      c.rcol[:, t: t + 1], wait=[dn], inc=("vec", 1))
            sc = P.op("vector", "tensor_scalar", c.oav[:, t * 128:(t + 1) * 128],
                      ba[:, 0:128], c.rcol[:, 4 + t: 5 + t], None, ALU.mult,
                      wait=[rc, c.oavfree[t]], inc=("vec", 1))
            av_free[t] = sc
            tp = P.op("tensor", "transpose", c.pbf[:, t * 128:(t + 1) * 128],
                      c.oav[:, t * 128:(t + 1) * 128], c.identb_sb[:, :],
                      wait=[sc, c.pbffree], inc=("pe", 1))
            ot.append(P.op("scalar", "activation",
                           c.outT[:, t * 128:(t + 1) * 128],
                           c.pbf[:, t * 128:(t + 1) * 128], AF.Copy,
                           wait=[tp, c.outTfree], inc=("act", 1)))
            c.oavfree[t] = tp
        c.pbffree = ot[-1]
        rj = s // 2
        wtok2 = P.dma("sync", "agw2",
                      c.ag2i[rj * 256 + hh * 128: rj * 256 + (hh + 1) * 128,
                             (s % 2) * 512: (s % 2) * 512 + 512],
                      c.outT[:, :], wait=ot)
        c.outTfree = wtok2
        out_toks.append(wtok2)
    return out_toks, ("pe", P.cnt["pe"])


def build_outproj(c, x2_sb, banks):
    P = c.P
    ld_ag = []
    for kc in range(NKC):
        ld_ag.append(P.dma("sync", f"ao{kc % 2}",
                           c.agown_sb[:, kc * T:(kc + 1) * T],
                           c.ag2o.rearrange("(c p) t -> p c t", p=128)[:, kc, :],
                           wait=[c.cc3_tok]))
    xload = P.dma("sync", "spl", x2_sb[:, :], c.xspill, wait=[c.spill_tok])
    prev_piece = [None, None]
    st_prev = {}
    bank_free = {}
    for dmq in range(4):
        pp = dmq % 2
        lw = P.dma("gpsimd", f"wo2{pp}", c.wout_buf[pp][:, :], c.woutp[dmq],
                   wait=[prev_piece[pp]])
        for dsub in range(4):
            dmg = dmq * 4 + dsub
            for th in range(2):
                par = (dsub * 2 + th) % 2
                bk = banks[par]
                for kc in range(NKC):
                    mm = P.mm(bk[:, :],
                              c.wout_buf[pp][:, kc * 512 + dsub * 128:
                                             kc * 512 + (dsub + 1) * 128],
                              c.agown_sb[:, kc * T + th * 512:
                                         kc * T + (th + 1) * 512],
                              start=(kc == 0), stop=(kc == NKC - 1),
                              wait=[lw, ld_ag[kc], bank_free.get(par),
                                    c.p4_guard],
                              inc=("pe", 1) if kc == NKC - 1 else None)
                sl = slice(dmg * T + th * 512, dmg * T + (th + 1) * 512)
                stt = P.op("vector", "tensor_tensor", x2_sb[:, sl], bk[:, :],
                           x2_sb[:, sl], ALU.add,
                           wait=[mm, xload, st_prev.get((dmg, th))],
                           inc=("vec", 1))
                st_prev[(dmg, th)] = stt
                bank_free[par] = stt
        prev_piece[pp] = ("pe", P.cnt["pe"])
    return [st_prev[(g2, th)] for g2 in range(16) for th in range(2)]



def _alloc_ffn_bufs(c, en):
    nc = c.nc
    u = c._uniq
    c._uniq += 1
    c.ln_scratch2 = [en(nc.sbuf_tensor(f"lt{u}_{i}", [128, 1024], F32)) for i in range(2)]
    c.stat_rows = [en(nc.sbuf_tensor(f"str{u}_{i}", [1, 1024], F32)) for i in range(4)]
    c.nm_b = en(nc.sbuf_tensor(f"nm_b{u}", [128, 1024], F32))
    c.rs_b = en(nc.sbuf_tensor(f"rs_b{u}", [128, 1024], F32))
    c.gate_sb = en(nc.sbuf_tensor(f"gate_sb{u}", [128, 64], F32))
    c.gcol_sb = en(nc.sbuf_tensor(f"gcol_sb{u}", [128, 8], F32))
    c.geq_sb = en(nc.sbuf_tensor(f"geq_sb{u}", [128, 32], F32))
    c.gres_sb = en(nc.sbuf_tensor(f"gres_sb{u}", [128, 64], F32))
    c.gT_sb = en(nc.sbuf_tensor(f"gT_sb{u}", [8, 1024], F32))


def _alloc_expert_bufs(c, en):
    nc = c.nc
    u = c._uniq
    c._uniq += 1
    c.ring = [en(nc.sbuf_tensor(f"ring{u}_{i}", [128, 4096], BF16)) for i in range(4)]
    c.ring_free = [None] * 4
    c.ring_i = [0]
    c.gb_sb = [en(nc.sbuf_tensor(f"gb{u}_{i}", [128, 1024], F32)) for i in range(2)]
    c.inter_sb = en(nc.sbuf_tensor(f"inter_sb{u}", [128, 8192], BF16))
    c.s_buf = [en(nc.sbuf_tensor(f"sbf{u}_{i}", [128, 512], F32)) for i in range(2)]
    c.hg_buf = [en(nc.sbuf_tensor(f"hgb{u}_{i}", [128, 512], F32)) for i in range(2)]
    c.s_free = [None, None]
    c.hg_free = [None, None]


def build_program():
    nc = bass.Bass("TRN2", target_bir_lowering=False)
    c = Ctx()
    c.nc = nc
    c._uniq = 0

    def dri(name, shape, dt=F32):
        return nc.dram_tensor(name, shape, dt, kind="ExternalInput").ap()

    c.x_in = dri("x_fm", [128, NKC * T])
    c.wi0p = dri("wi0p", [E, 4, 128, 16 * 256])
    c.wi1p = dri("wi1p", [E, 4, 128, 16 * 256])
    c.wop = dri("wop", [E, 4, 128, 8 * 512])
    c.wg = dri("wgp", [128, NKC * 8])
    c.wdqp = dri("wdqp", [128, NKC * 1024])
    c.wdkvp = dri("wdkvp", [128, NKC * 512])
    c.wkrp = dri("wkrp", [128, NKC * 64])
    c.wukp = dri("wukp", [2, 128, 4 * 128])
    c.wuvp = dri("wuvp", [2, 128, 4 * 128])
    c.wuqp = dri("wuqp", [2, 128, 8 * 128])
    c.wqrp = dri("wqrp", [2, 128, 8 * 64])
    c.woutp = dri("woutp", [4, 128, 16 * 512])
    c.delta = dri("delta", [32, 1])
    c.cosA = dri("cosA", [32, S])
    c.sinA = dri("sinA", [32, S])
    c.maskT = dri("maskT", [128, 128])
    c.identity = dri("ident", [128, 128])
    c.identb = dri("identb", [128, 128])
    c.ones2d = dri("ones2d", [128, 128])
    c.basis = dri("basis", [8, 1024])
    c.out = nc.dram_tensor("out", [D, T], F32, kind="ExternalOutput").ap()

    c.xspill = nc.dram_tensor("xspill", [128, NKC * T], F32).ap()
    c.ag1i = nc.dram_tensor("ag1i", [1600, 1024], BF16).ap()
    c.ag1o = nc.dram_tensor("ag1o", [8 * 1600, 1024], BF16).ap()
    c.ag2i = nc.dram_tensor("ag2i", [2048, 1024], BF16).ap()
    c.ag2o = nc.dram_tensor("ag2o", [2048, 1024], BF16).ap()

    RG = [list(range(NCORE))]

    with ExitStack() as stack:
        P = Prog(nc, stack, dma_streams=[
            ("din", "batch"), ("dinb", "batch"), ("dout", "batch"),
            ("r0", "serial"), ("r1", "serial"), ("r2", "serial"), ("r3", "serial"),
            ("wat", "batch"), ("wat2", "batch"), ("ddl", "batch"),
            ("spl", "serial"), ("agw1", "batch"), ("agw2", "batch"),
            ("ang0", "serial"), ("ang1", "serial"),
            ("cq0", "serial"), ("cq1", "serial"),
            ("ck0", "serial"), ("ck1", "serial"),
            ("krs0", "serial"), ("krs1", "serial"),
            ("ao0", "serial"), ("ao1", "serial"),
            ("rp0", "serial"), ("rp1", "serial"),
            ("wo20", "serial"), ("wo21", "serial"),
            ("pfk0", "serial"), ("pfk1", "serial"),
            ("pfq0", "serial"), ("pfq1", "serial"),
            ("wat3", "batch"),
        ])
        c.P = P
        en = stack.enter_context

        c.ones2d_sb = en(nc.sbuf_tensor("ones2d_sb", [128, 128], F32))
        c.ident_sb = en(nc.sbuf_tensor("ident_sb", [128, 128], F32))
        c.identb_sb = en(nc.sbuf_tensor("identb_sb", [128, 128], BF16))
        c.maskT_sb = en(nc.sbuf_tensor("maskT_sb", [128, 128], F32))
        c.basis_sb = en(nc.sbuf_tensor("basis_sb", [8, 1024], F32))
        c.wg_sb = en(nc.sbuf_tensor("wg_sb", [128, NKC * 8], F32))
        c.eps_sb = en(nc.sbuf_tensor("eps_sb", [1, 8], F32))
        c.clog_sb = en(nc.sbuf_tensor("clog_sb", [1, 8], F32))
        c.dt_sb = en(nc.sbuf_tensor("dt_sb", [32, 8], F32))
        c.dti_sb = en(nc.sbuf_tensor("dti_sb", [32, 1], I32))

        P.dma("sync", "din", c.ones2d_sb[:, :], c.ones2d)
        P.dma("sync", "din", c.ident_sb[:, :], c.identity)
        P.dma("gpsimd", "dinb", c.identb_sb[:, :], c.identb)
        P.dma("sync", "din", c.maskT_sb[:, :], c.maskT)
        P.dma("sync", "din", c.basis_sb[:, :], c.basis)
        P.dma("sync", "din", c.wg_sb[:, :], c.wg)
        c.eps_tok = P.op("vector", "memset", c.eps_sb[0:1, :], EPS, inc=("vec", 1))
        c.clog_tok = P.op("vector", "memset", c.clog_sb[0:1, :], CLOG, inc=("vec", 1))
        ctot = ("din", P.cnt["din"])
        ctotb = ("dinb", P.cnt["dinb"])
        for _e in ENGS:
            P.op(_e, "_waits_only", wait=[ctot, ctotb])

        with ExitStack() as p12:
          c.xn_sb = p12.enter_context(nc.sbuf_tensor("xn_sb", [128, NKC * T], BF16))
          with ExitStack() as p1:
            e1 = p1.enter_context
            x_sb = e1(nc.sbuf_tensor("x_sb", [128, NKC * T], F32))
            _alloc_ffn_bufs(c, e1)
            banks = [e1(nc.psum_tensor(f"pb{i}", [128, 512], F32)) for i in range(8)]
            c.x_ready = P.dma("sync", "din", x_sb[:, :], c.x_in)
            c.bank_guard = {}
            with ExitStack() as pf:
                _alloc_expert_bufs(c, pf.enter_context)
                last_stt = build_ffn(c, x_sb, c.x_ready, banks)
            xr2 = ("vec", max(v for s_, v in last_stt))
            c.wdq_sb = p12.enter_context(
                nc.sbuf_tensor("wdq_sb", [128, NKC * 1024], BF16, side="right"))
            c.wdkv_sb = p12.enter_context(
                nc.sbuf_tensor("wdkv_sb", [128, NKC * 512], BF16, side="right"))
            c.wkr_sb = p12.enter_context(
                nc.sbuf_tensor("wkr_sb", [128, NKC * 64], BF16, side="right"))
            P.dma("gpsimd", "wat", c.wdq_sb[:, :], c.wdqp, wait=[xr2])
            P.dma("gpsimd", "wat", c.wdkv_sb[:, :], c.wdkvp, wait=[xr2])
            c.wat_tok = P.dma("gpsimd", "wat", c.wkr_sb[:, :], c.wkrp,
                              wait=[xr2])
            xn2_toks, _ = build_ln(c, x_sb, xr2, banks, with_gate=False)
            c.spill_tok = P.dma("sync", "spl", c.xspill, x_sb[:, :],
                                wait=[xn2_toks[NKC - 1], c.ln_stat_done])
          P.barrier()
          with ExitStack() as p2:
            e2 = p2.enter_context
            c.cq_sb = e2(nc.sbuf_tensor("cq_sb", [128, 8 * T], BF16))
            c.ckv_sb = e2(nc.sbuf_tensor("ckv_sb", [128, 4 * T], BF16))
            c.kr_sb = e2(nc.sbuf_tensor("kr_sb", [64, T], BF16))
            banks2 = [e2(nc.psum_tensor(f"pc{i}", [128, 512], F32)) for i in range(6)]
            agw_ckv, agw_kr, agw_cq = build_proj_only(c, xn2_toks, banks2)
        P.barrier()

        with ExitStack() as p3:
            e3 = p3.enter_context
            banks = [e3(nc.psum_tensor(f"qb{i}", [128, 512], F32)) for i in range(7)]
            c.pbf = e3(nc.psum_tensor("pbf", [128, 1024], BF16))
            c.avbanks = [banks[2], banks[3], banks[4], banks[5]]
            c.pope_scr = e3(nc.sbuf_tensor("pope_scr", [32, 2048], BF16))
            c.pope_bf = e3(nc.sbuf_tensor("pope_bf", [32, 4096], BF16))
            c.cosAng = e3(nc.sbuf_tensor("cosAng", [32, S], BF16))
            c.sinAng = e3(nc.sbuf_tensor("sinAng", [32, S], BF16))
            c.angbt = e3(nc.sbuf_tensor("angbt", [32, 1024], BF16))
            c.angb = [c.angbt[0:32, 0:1024], c.angbt[0:32, 0:1024]]
            c.kcbuf = e3(nc.sbuf_tensor("kcbuf", [128, S], BF16))
            c.krp = e3(nc.sbuf_tensor("krp", [128, S], BF16))
            c.qcbuf = e3(nc.sbuf_tensor("qcbuf", [128, S], BF16))
            c.qrp = e3(nc.sbuf_tensor("qrp", [128, S], BF16))
            c.rp_stage = e3(nc.sbuf_tensor("rp_stage", [32, 1024], BF16))
            c.vaug_sb = e3(nc.sbuf_tensor("vaug", [128, 64 * 129], BF16))
            c.cqb = [e3(nc.sbuf_tensor(f"cqb{i}", [128, 8 * 512], BF16)) for i in range(2)]
            c.ckb = [e3(nc.sbuf_tensor(f"ckb{i}", [128, 4 * 512], BF16)) for i in range(2)]
            c.krb0 = e3(nc.sbuf_tensor("krb0", [32, 2048], BF16))
            c.krb = [c.krb0, c.krb0]
            c.qraw = e3(nc.sbuf_tensor("qraw", [32, 2048], BF16))
            c.sqn = e3(nc.sbuf_tensor("sqn", [128, 512], BF16))
            c.sqk = e3(nc.sbuf_tensor("sqk", [128, 512], BF16))
            c.onesb = e3(nc.sbuf_tensor("onesb", [128, 8], BF16))
            c.sqr = e3(nc.sbuf_tensor("sqr", [64, 512], BF16))
            c.mxp = e3(nc.sbuf_tensor("mxp", [1, 48], F32))
            c.mxq = e3(nc.sbuf_tensor("mxq", [1, 8], F32))
            c.krmax = e3(nc.sbuf_tensor("krmax", [1, 1], F32))
            c.negc_sb = e3(nc.sbuf_tensor("negc", [128, 8], F32))
            c.probs = [e3(nc.sbuf_tensor(f"probs{i}", [128, 512], BF16)) for i in range(2)]
            c.oav = e3(nc.sbuf_tensor("oav", [128, 512], BF16))
            c.outT = e3(nc.sbuf_tensor("outT", [128, 512], BF16))
            c.rcol = e3(nc.sbuf_tensor("rcol", [128, 8], F32))
            wuk1 = e3(nc.sbuf_tensor("wuk", [128, 4 * 128], BF16))
            wuv1 = e3(nc.sbuf_tensor("wuv", [128, 4 * 128], BF16))
            wuq1 = e3(nc.sbuf_tensor("wuq", [128, 8 * 128], BF16))
            wqr1 = e3(nc.sbuf_tensor("wqr", [128, 8 * 64], BF16))
            wuk2 = e3(nc.sbuf_tensor("wuk2", [128, 4 * 128], BF16))
            wuv2 = e3(nc.sbuf_tensor("wuv2", [128, 4 * 128], BF16))
            wuq2 = e3(nc.sbuf_tensor("wuq2", [128, 8 * 128], BF16))
            wqr2 = e3(nc.sbuf_tensor("wqr2", [128, 8 * 64], BF16))
            c.wuk_sb = [wuk1, wuk2]
            c.wuv_sb = [wuv1, wuv2]
            c.wuq_sb = [wuq1, wuq2]
            c.wqr_sb = [wqr1, wqr2]
            c.cqb3 = e3(nc.sbuf_tensor("cqb3", [128, 2 * 8 * 512], BF16))
            c.ckb3 = e3(nc.sbuf_tensor("ckb3", [128, 2 * 4 * 512], BF16))

            P.barrier()
            P.dma("gpsimd", "wat2", c.wuk_sb[0][:, :], c.wukp[0])
            P.dma("gpsimd", "wat2", c.wuv_sb[0][:, :], c.wuvp[0])
            P.dma("gpsimd", "wat2", c.wuq_sb[0][:, :], c.wuqp[0])
            c.wat2_tok = P.dma("gpsimd", "wat2", c.wqr_sb[0][:, :], c.wqrp[0])
            P.dma("gpsimd", "wat3", c.wuk_sb[1][:, :], c.wukp[1])
            P.dma("gpsimd", "wat3", c.wuv_sb[1][:, :], c.wuvp[1])
            P.dma("gpsimd", "wat3", c.wuq_sb[1][:, :], c.wuqp[1])
            c.wat3_tok = P.dma("gpsimd", "wat3", c.wqr_sb[1][:, :], c.wqrp[1])
            c.cc1_tok = P.op("gpsimd", "collective_compute", "AllGather",
                             ALU.bypass, RG, ins=[c.ag1i], outs=[c.ag1o],
                             wait=[agw_ckv, agw_kr, agw_cq], inc=("cc", 1))
            c.cc1a_tok = c.cc1_tok
            c.cc2_tok = c.cc1_tok
            c.trig_toks = build_delta_trig(c)
            c.pope_free = None
            c.rp_free = [None, None]
            c.pope_i = [0]
            c.gfree = None
            c.sqn_free = None
            c.sqr_free = None
            c.sqk_free = None
            build_angle_tables(c, banks)
            c.onesb_tok = P.op("vector", "memset", c.onesb[:, :], 1.0,
                               inc=("vec", 1))
            msk1 = P.op("vector", "memset", c.krp[:, :], 0.0, inc=("vec", 1))
            msk2 = P.op("vector", "memset", c.qrp[:, :], 0.0, inc=("vec", 1))
            c.pack_ms = msk2
            c.pope_free = [c.ang_done, c.ang_done]
            c.kr_toks = {}
            c.kr_nmax = []
            c.kr_prevs = [None, None]
            c.kcfree = [None, None]
            c.vfree = [None, None]
            c.qcfree = [None, None]
            c.qrawfree = [None, None]
            c.scfree = [None, None]
            c.probs_free = [None, None]
            c.av_free = [None] * 4
            c.oavfree = [None] * 4
            c.pbffree = None
            c.outTfree = None
            hg0 = None
            all_out = []
            for hh in range(2):
                if hh == 1:
                    c.wat2_tok = c.wat3_tok
                out_toks, pe_end = build_head(c, hh, banks, hg0)
                hg0 = pe_end
                all_out += out_toks
            c.cc3_tok = P.op("gpsimd", "collective_compute", "AllToAll",
                             ALU.bypass, RG, ins=[c.ag2i], outs=[c.ag2o],
                             wait=[c.cc2_tok, ("agw2", P.cnt["agw2"])],
                             inc=("cc", 1))
            P.barrier()

        with ExitStack() as p4:
            e4 = p4.enter_context
            x2_sb = e4(nc.sbuf_tensor("x2_sb", [128, NKC * T], F32))
            banks = [e4(nc.psum_tensor(f"rb{i}", [128, 512], F32)) for i in range(8)]
            c.p4_guard = None
            with ExitStack() as po:
                eo = po.enter_context
                c.agown_sb = eo(nc.sbuf_tensor("agown", [128, NKC * T], BF16))
                c.wout_buf = [eo(nc.sbuf_tensor(f"woutb{i}", [128, 16 * 512], BF16))
                              for i in range(2)]
                op_stt = build_outproj(c, x2_sb, banks)
            P.barrier()
            _alloc_ffn_bufs(c, e4)
            c.xn_sb = e4(nc.sbuf_tensor("xn2_sb", [128, NKC * T], BF16))
            with ExitStack() as pf2:
                _alloc_expert_bufs(c, pf2.enter_context)
                c.bank_guard = {}
                xr4 = ("vec", max(v for s_, v in op_stt))
                last2 = build_ffn(c, x2_sb, xr4, banks)
            out_tok = P.dma("sync", "dout",
                            c.out.rearrange("(c p) t -> p c t", p=128),
                            x2_sb.ap().rearrange("p (c t) -> p c t", c=NKC),
                            wait=last2)
            P.op("sync", "_waits_only", wait=[out_tok])

        with nc.Block() as block:
            P.replay(block)
    return nc



def prep_inputs(L0_attn_x, L0_attn_w_dkv, L0_attn_w_uk, L0_attn_w_uv,
                L0_attn_w_dq, L0_attn_w_uq, L0_attn_w_qr, L0_attn_w_kr,
                L0_attn_w_out, L0_attn_delta, L0_ffn_w_gate, L0_ffn_wi_0,
                L0_ffn_wi_1, L0_ffn_wo):
    cosA, sinA, maskT, ident, identb, ones2d, basis = make_consts()
    x = np.asarray(L0_attn_x)[0]
    xT = np.ascontiguousarray(x.T)
    wi0 = np.asarray(L0_ffn_wi_0)
    wi1 = np.asarray(L0_ffn_wi_1)
    wo = np.asarray(L0_ffn_wo)
    wi0p = np.stack([np.stack([chunk_pf(wi0[e][:, mq * 256:(mq + 1) * 256])
                               for mq in range(4)]) for e in range(E)])
    wi1p = np.stack([np.stack([chunk_pf(wi1[e][:, mq * 256:(mq + 1) * 256])
                               for mq in range(4)]) for e in range(E)])
    wop = np.stack([np.stack([chunk_pf(wo[e][:, dq * 512:(dq + 1) * 512])
                              for dq in range(4)]) for e in range(E)])
    wgp = chunk_pf(np.asarray(L0_ffn_w_gate))
    wdqp = chunk_pf(np.asarray(L0_attn_w_dq))
    wdkvp = chunk_pf(np.asarray(L0_attn_w_dkv))
    wkrp = chunk_pf(np.asarray(L0_attn_w_kr))
    w_uk = np.asarray(L0_attn_w_uk)
    w_uv = np.asarray(L0_attn_w_uv)
    w_uq = np.asarray(L0_attn_w_uq)
    w_qr = np.asarray(L0_attn_w_qr)
    woutp = np.stack([chunk_pf(np.asarray(L0_attn_w_out)[:, dmq * 512:(dmq + 1) * 512])
                      for dmq in range(4)])
    delta = np.ascontiguousarray(np.asarray(L0_attn_delta).reshape(32, 1))
    shared = {"wi0p": wi0p, "wi1p": wi1p, "wop": wop, "wgp": wgp,
              "wdqp": wdqp, "wdkvp": wdkvp, "wkrp": wkrp, "woutp": woutp,
              "delta": delta, "cosA": cosA, "sinA": sinA, "maskT": maskT,
              "ident": ident, "identb": identb, "ones2d": ones2d, "basis": basis}
    in_maps = []
    for cid in range(NCORE):
        xs = np.ascontiguousarray(xT[:, cid * T:(cid + 1) * T])
        m = dict(shared)
        m["x_fm"] = chunk_pf(xs)
        m["wukp"] = np.stack([chunk_pf(w_uk[:, (2 * cid + hh) * 128:
                                            (2 * cid + hh + 1) * 128])
                              for hh in range(2)])
        m["wuvp"] = np.stack([chunk_pf(w_uv[:, (2 * cid + hh) * 128:
                                            (2 * cid + hh + 1) * 128])
                              for hh in range(2)])
        m["wuqp"] = np.stack([chunk_pf(w_uq[:, (2 * cid + hh) * 128:
                                            (2 * cid + hh + 1) * 128])
                              for hh in range(2)])
        m["wqrp"] = np.stack([chunk_pf(w_qr[:, (2 * cid + hh) * 64:
                                            (2 * cid + hh + 1) * 64])
                              for hh in range(2)])
        in_maps.append(m)
    return in_maps


def kernel(L0_attn_x, L0_attn_w_dkv, L0_attn_w_uk, L0_attn_w_uv, L0_attn_w_dq,
           L0_attn_w_uq, L0_attn_w_qr, L0_attn_w_kr, L0_attn_w_out,
           L0_attn_delta, L0_ffn_w_gate, L0_ffn_wi_0, L0_ffn_wi_1, L0_ffn_wo):
    in_maps = prep_inputs(L0_attn_x, L0_attn_w_dkv, L0_attn_w_uk, L0_attn_w_uv,
                          L0_attn_w_dq, L0_attn_w_uq, L0_attn_w_qr, L0_attn_w_kr,
                          L0_attn_w_out, L0_attn_delta, L0_ffn_w_gate,
                          L0_ffn_wi_0, L0_ffn_wi_1, L0_ffn_wo)
    nc = build_program()
    res = run_bass_kernel_spmd(nc, in_maps, core_ids=list(range(NCORE)))
    shards = [res.results[cid]["out"].T for cid in range(NCORE)]
    return np.concatenate(shards, axis=0)[None].astype(np.float32)


# version 27 (final)
# speedup vs baseline: 1.0018x; 1.0018x over previous
import math
import os
from contextlib import ExitStack

os.environ.setdefault("TRN_ALLOW_ICI_ALL_TO_ALL", "1")

import numpy as np
import concourse.bass as bass
import concourse.mybir as mybir
from concourse.bass_utils import run_bass_kernel_spmd

F32 = mybir.dt.float32
BF16 = mybir.dt.bfloat16
I32 = mybir.dt.int32
AF = mybir.ActivationFunctionType
ALU = mybir.AluOpType
AX = mybir.AxisListType.X

NCORE = 8
D = 2048
S = 8192
T = S // NCORE
NKC = D // 128
E = 8
M = 1024
NH = 16
HPC = 2
DH = 128
DR = 64
DCKV = 512
DCQ = 1024
SCALE = (DH + DR) ** -0.5
EPS = 1e-5
NEGBIG = -1.0e30
TWO_PI = 2.0 * math.pi
CLOG = math.log(1.02 * SCALE)

ENGS = ("tensor", "vector", "scalar", "gpsimd", "sync")


class Prog:

    COMPUTE_SEMS = ("pe", "vec", "act", "pool", "cc", "bar")

    def __init__(self, nc, stack, dma_streams):
        self.nc = nc
        self.rec = {e: [] for e in ENGS}
        self.sem = {}
        self.cnt = {}
        self.seen = {e: {} for e in ENGS}
        self.serial = {}
        self.last_tok = {e: [] for e in ENGS}
        self._hazard = {e: False for e in ENGS}
        for name in self.COMPUTE_SEMS:
            self._add_sem(stack, name)
        for name, mode in dma_streams:
            self._add_sem(stack, name)
            self.serial[name] = (mode == "serial")

    def _add_sem(self, stack, name):
        self.sem[name] = stack.enter_context(self.nc.semaphore(name))
        self.cnt[name] = 0
        for e in ENGS:
            self.seen[e][name] = 0

    SELF_SER = {"vector": "vec", "scalar": "act"}
    HAZARD_METHS = ("tensor_reduce", "reduce_max", "reduce_sum", "memset",
                    "reciprocal")

    def op(self, eng, meth, *args, wait=(), inc=None, **kw):
        wait = list(wait)
        ss = self.SELF_SER.get(eng)
        if ss is not None and meth != "_waits_only":
            if self._hazard.get(eng):
                wait.append((ss, self.cnt[ss]))
                self._hazard[eng] = False
            if meth in self.HAZARD_METHS:
                if inc is None:
                    inc = (ss, 1)
                self._hazard[eng] = True
        waits = []
        for w in wait:
            if w is None:
                continue
            s, v = w
            if v > self.seen[eng][s]:
                self.seen[eng][s] = v
                waits.append((s, v))
        tok = None
        if inc is not None:
            s, d = inc
            self.cnt[s] += d
            tok = (s, self.cnt[s])
            self.last_tok[eng] = [tok]
        self.rec[eng].append((meth, args, kw, tuple(waits),
                              (inc[0], inc[1]) if inc else None))
        return tok

    def dma(self, eng, stream, out, in_, wait=(), **kw):
        w = list(wait)
        if self.serial[stream]:
            w.append((stream, self.cnt[stream]))
        return self.op(eng, "dma_start", out=out, in_=in_, wait=w,
                       inc=(stream, 16), **kw)

    def mm(self, out, lhsT, rhs, start, stop, wait=(), inc=None):
        return self.op("tensor", "matmul", out, lhsT, rhs, start=start, stop=stop,
                       wait=wait, inc=inc)

    def barrier(self):
        toks = []
        for e in ENGS:
            toks += self.last_tok[e]
        for e in ENGS:
            self.op(e, "_waits_only", wait=toks)
        return toks

    def check(self):
        final = dict(self.cnt)
        inc_by = {}
        for e in ENGS:
            for meth, args, kw, waits, inc in self.rec[e]:
                if inc:
                    inc_by.setdefault(inc[0], {}).setdefault(e, 0)
                    inc_by[inc[0]][e] += inc[1]
        for e in ENGS:
            own = {}
            for i, (meth, args, kw, waits, inc) in enumerate(self.rec[e]):
                for (s, v) in (waits if meth != "_waits_only" else waits):
                    assert v <= final.get(s, 0), \
                        f"unreachable wait {s}>={v} (final {final.get(s)}) on {e} op {i} {meth}"
                    others = final.get(s, 0) - inc_by.get(s, {}).get(e, 0)
                    need_own = v - others
                    assert need_own <= own.get(s, 0), \
                        f"self-starvation: {e} op {i} {meth} waits {s}>={v}; " \
                        f"needs {need_own} own incs but only {own.get(s, 0)} so far"
                if meth == "_waits_only":
                    for (s, v) in args:
                        pass
                if inc:
                    own[inc[0]] = own.get(inc[0], 0) + inc[1]

    def replay(self, block):
        self.check()
        sems = self.sem
        for eng_name in ENGS:
            ops = self.rec[eng_name]

            def make(ops=ops):
                def body(e):
                    for meth, args, kw, waits, inc in ops:
                        for (s, v) in waits:
                            e.wait_ge(sems[s], v)
                        if meth == "_waits_only":
                            continue
                        r = getattr(e, meth)(*args, **kw)
                        if inc is not None:
                            r.then_inc(sems[inc[0]], inc[1])
                return body

            getattr(block, eng_name)(make())



def chunk_pf(w):
    n, m = w.shape
    assert n % 128 == 0, (n, m)
    return np.ascontiguousarray(
        w.reshape(n // 128, 128, m).transpose(1, 0, 2).reshape(128, -1))


def make_consts():
    half = DR // 2
    theta = 1.0 / (10000.0 ** (2.0 * np.arange(half, dtype=np.float64) / DR))
    ang = np.arange(S, dtype=np.float64)[None, :] * theta[:, None]
    cosA = np.cos(ang).astype(np.float32)
    sinA = np.sin(ang).astype(np.float32)
    k = np.arange(128)
    maskT = np.where(k[:, None] > k[None, :], np.float32(NEGBIG),
                     np.float32(0.0))
    ident = np.eye(128, dtype=np.float32)
    identb = np.eye(128, dtype=np.float32)
    ones2d = np.ones((128, 128), dtype=np.float32)
    basis = np.zeros((8, 1024), dtype=np.float32)
    for e in range(8):
        basis[e, e * 128:(e + 1) * 128] = 1.0
    return cosA, sinA, maskT, ident, identb, ones2d, basis


class Ctx:
    pass



def build_ln(c, x_sb, x_ready, banks, with_gate):
    P = c.P
    ones_col = c.ones2d_sb[:, 0:1]
    sq = [c.ln_scratch2[0][:, 0:512], c.ln_scratch2[1][:, 0:512]]
    sq_tok = [None, None]
    mm_after_sq = [None, None]
    stat_done = []
    for kc in range(NKC):
        for th in range(2):
            idx = (kc * 2 + th) % 2
            xs = x_sb[:, kc * T + th * 512: kc * T + (th + 1) * 512]
            sq_tok[idx] = P.op("scalar", "activation", sq[idx], xs, AF.Square,
                               wait=[x_ready, mm_after_sq[idx]], inc=("act", 1))
            P.mm(banks[th][0:1, :], ones_col, xs, start=(kc == 0), stop=(kc == NKC - 1),
                 wait=[x_ready, c.bank_guard.get(th)])
            tok = P.mm(banks[2 + th][0:1, :], ones_col, sq[idx],
                       start=(kc == 0), stop=(kc == NKC - 1),
                       wait=[sq_tok[idx], c.bank_guard.get(2 + th)], inc=("pe", 1))
            mm_after_sq[idx] = tok
            if kc == NKC - 1:
                stat_done.append(tok)
    r0, r1, r2, r3 = c.stat_rows
    sums, sumsq, mean, negmean = r0, r1, r2, r3
    c.ln_stat_done = stat_done[-1]
    ev = []
    for th in range(2):
        sl = slice(th * 512, (th + 1) * 512)
        ev.append(P.op("vector", "tensor_copy", sums[0:1, sl], banks[th][0:1, :],
                       wait=stat_done, inc=("vec", 1)))
        ev.append(P.op("vector", "tensor_copy", sumsq[0:1, sl], banks[2 + th][0:1, :],
                       wait=stat_done, inc=("vec", 1)))
    t0 = P.op("vector", "tensor_scalar", mean[0:1, :], sums[0:1, :], 1.0 / D, None,
              ALU.mult, wait=ev, inc=("vec", 1))
    t1 = P.op("vector", "tensor_scalar", negmean[0:1, :], sums[0:1, :], -1.0 / D, None,
              ALU.mult, wait=ev, inc=("vec", 1))
    t2 = P.op("vector", "tensor_scalar", r0[0:1, :], sumsq[0:1, :], 1.0 / D, None,
              ALU.mult, wait=ev + [t0, t1], inc=("vec", 1))
    t3 = P.op("vector", "tensor_tensor", r1[0:1, :], mean[0:1, :], mean[0:1, :],
              ALU.mult, wait=[t0, t2], inc=("vec", 1))
    t4 = P.op("vector", "tensor_tensor", r0[0:1, :], r0[0:1, :], r1[0:1, :],
              ALU.subtract, wait=[t3], inc=("vec", 1))
    t5 = P.op("scalar", "activation", r1[0:1, :], r0[0:1, :], AF.Ln,
              bias=c.eps_sb[0:1, 0:1], wait=[t4, c.eps_tok], inc=("act", 1))
    t6 = P.op("scalar", "activation", r2[0:1, :], r1[0:1, :], AF.Exp, scale=-0.5,
              wait=[t5], inc=("act", 1))
    rstd = r2
    ones_row = c.ones2d_sb[0:1, 0:128]
    btoks = []
    prev = None
    for th in range(2):
        sl = slice(th * 512, (th + 1) * 512)
        tokm = P.mm(banks[0][:, :], ones_row, negmean[0:1, sl], start=True, stop=True,
                    wait=[t1] + ev + ([prev] if prev else []), inc=("pe", 1))
        cm = P.op("vector", "tensor_copy", c.nm_b[:, sl], banks[0][:, :],
                  wait=[tokm], inc=("vec", 1))
        tokr = P.mm(banks[1][:, :], ones_row, rstd[0:1, sl], start=True, stop=True,
                    wait=[t6] + ev + ([prev] if prev else []), inc=("pe", 1))
        cr = P.op("vector", "tensor_copy", c.rs_b[:, sl], banks[1][:, :],
                  wait=[tokr], inc=("vec", 1))
        btoks += [cm, cr]
        prev = cr
    bc_done = btoks[-1]

    lt = c.ln_scratch2
    lnf = lt
    xn_toks = {}
    lnf_consumed = [None, None]
    gate_last = [None] * 8
    for kc in range(NKC):
        idx = kc % 2
        sl = slice(kc * T, kc * T + 1024)
        a = P.op("vector", "tensor_tensor", lt[idx][:, :], x_sb[:, sl], c.nm_b[:, 0:1024],
                 ALU.add,
                 wait=[bc_done, x_ready, stat_done[-1]] + (lnf_consumed[idx] or []),
                 inc=("vec", 1))
        b = P.op("vector", "tensor_tensor", lnf[idx][:, :], lt[idx][:, :],
                 c.rs_b[:, 0:1024], ALU.mult, wait=[a], inc=("vec", 1))
        xt = P.op("scalar", "activation", c.xn_sb[:, sl], lnf[idx][:, :], AF.Copy,
                  wait=[b], inc=("act", 1))
        xn_toks[kc] = xt
        if with_gate:
            for tc in range(8):
                w0 = [b] + (btoks + ev + [c.bank_guard.get(tc)] if kc == 0 else [])
                gate_last[tc] = P.mm(banks[tc][0:128, 0:8],
                                     lnf[idx][:, tc * 128:(tc + 1) * 128],
                                     c.wg_sb[:, kc * 8:(kc + 1) * 8],
                                     start=(kc == 0), stop=(kc == NKC - 1), wait=w0,
                                     inc=("pe", 1))
            lnf_consumed[idx] = [gate_last[7], xt]
        else:
            lnf_consumed[idx] = [xt]

    if not with_gate:
        return xn_toks, None

    gT_toks = []
    prev_p = None
    for tc in range(8):
        l8 = c.gate_sb[:, tc * 8:(tc + 1) * 8]
        ev2 = P.op("vector", "tensor_copy", l8, banks[tc][0:128, 0:8],
                   wait=[gate_last[tc]], inc=("vec", 1))
        col = c.gcol_sb
        m1 = P.op("vector", "reduce_max", col[:, 0:1], l8, axis=AX, wait=[ev2],
                  inc=("vec", 1))
        eq = P.op("vector", "tensor_scalar", c.geq_sb[:, 0:8], l8, col[:, 0:1], None,
                  ALU.is_equal, wait=[m1], inc=("vec", 1))
        l2 = P.op("vector", "scalar_tensor_tensor", c.geq_sb[:, 8:16],
                  c.geq_sb[:, 0:8], NEGBIG, l8, ALU.mult, ALU.add,
                  wait=[eq, prev_p], inc=("vec", 1))
        m2 = P.op("vector", "reduce_max", col[:, 1:2], c.geq_sb[:, 8:16], axis=AX,
                  wait=[l2], inc=("vec", 1))
        ge = P.op("vector", "tensor_scalar", c.geq_sb[:, 0:8], l8, col[:, 1:2], None,
                  ALU.is_ge, wait=[m2], inc=("vec", 1))
        nm1 = P.op("vector", "tensor_scalar", col[:, 2:3], col[:, 0:1], -1.0, None,
                   ALU.mult, wait=[m1, prev_p], inc=("vec", 1))
        p = P.op("scalar", "activation", c.geq_sb[:, 16:24], l8, AF.Exp,
                 bias=col[:, 2:3], wait=[nm1, ge], inc=("act", 1))
        prev_p = p
        pm = P.op("vector", "tensor_tensor", c.geq_sb[:, 24:32], c.geq_sb[:, 16:24],
                  c.geq_sb[:, 0:8], ALU.mult, wait=[p], inc=("vec", 1))
        sm = P.op("vector", "reduce_sum", col[:, 3:4], c.geq_sb[:, 24:32], axis=AX,
                  wait=[pm], inc=("vec", 1))
        rs = P.op("vector", "reciprocal", col[:, 4:5], col[:, 3:4], wait=[sm],
                  inc=("vec", 1))
        g = P.op("vector", "tensor_scalar", c.gres_sb[:, tc * 8:(tc + 1) * 8],
                 c.geq_sb[:, 24:32], col[:, 4:5], None, ALU.mult, wait=[rs],
                 inc=("vec", 1))
        tk = P.op("tensor", "transpose", banks[tc][0:8, 0:128],
                  c.gres_sb[:, tc * 8:(tc + 1) * 8], c.ident_sb[:, :],
                  wait=[g], inc=("pe", 1))
        gT_toks.append(P.op("vector", "tensor_copy",
                            c.gT_sb[0:8, tc * 128:(tc + 1) * 128],
                            banks[tc][0:8, 0:128], wait=[tk], inc=("vec", 1)))
    return xn_toks, gT_toks



def build_ffn(c, x_sb, x_ready, banks):
    P = c.P
    xn_toks, gT_toks = build_ln(c, x_sb, x_ready, banks, with_gate=True)

    ring = c.ring
    ring_free = c.ring_free
    ring_i = c.ring_i

    def load_piece(dram_ap, extra=()):
        i = ring_i[0] % 4
        ring_i[0] += 1
        tok = P.dma("gpsimd", f"r{i}", ring[i][:, :], dram_ap,
                    wait=[ring_free[i]] + list(extra))
        return i, tok

    bank_free = {i: gT_toks[i] for i in range(8)}
    gb_tok = [None, None]
    st_prev = {}

    for e in range(E):
        gbi = e % 2
        gb_ready = []
        for th in range(2):
            tok = P.mm(banks[6][:, :], c.basis_sb[0:8, e * 128:(e + 1) * 128],
                       c.gT_sb[0:8, th * 512:(th + 1) * 512], start=True, stop=True,
                       wait=gT_toks + [gb_tok[gbi], bank_free[6]], inc=("pe", 1))
            cpy = P.op("vector", "tensor_copy", c.gb_sb[gbi][:, th * 512:(th + 1) * 512],
                       banks[6][:, :], wait=[tok], inc=("vec", 1))
            bank_free[6] = cpy
            gb_ready.append(cpy)
        gb_tok[gbi] = gb_ready[-1]

        p_wi = {}
        p_wi[(0, 0)] = load_piece(c.wi0p[e, 0])
        p_wi[(1, 0)] = load_piece(c.wi1p[e, 0])
        p_wi[(0, 1)] = load_piece(c.wi0p[e, 1])
        p_wi[(1, 1)] = load_piece(c.wi1p[e, 1])

        inter_toks = []
        wo_pieces = []
        for mq in range(4):
            s0, t0 = p_wi[(0, mq)]
            s1, t1 = p_wi[(1, mq)]
            for mi in range(2):
                m = mq * 2 + mi
                pA = [banks[0], banks[1]] if mi == 0 else [banks[2], banks[3]]
                pB = [banks[2], banks[3]] if mi == 0 else [banks[0], banks[1]]
                for kc in range(NKC):
                    lhs = ring[s0][:, kc * 256 + mi * 128: kc * 256 + (mi + 1) * 128]
                    P.mm(pA[0][:, :], lhs,
                         c.xn_sb[:, kc * T: kc * T + 512],
                         start=(kc == 0), stop=(kc == NKC - 1),
                         wait=[t0, xn_toks[kc], bank_free[0 if mi == 0 else 2]])
                    h0 = P.mm(pA[1][:, :], lhs,
                              c.xn_sb[:, kc * T + 512: kc * T + 1024],
                              start=(kc == 0), stop=(kc == NKC - 1),
                              wait=[bank_free[1 if mi == 0 else 3]],
                              inc=("pe", 1) if kc == NKC - 1 else None)
                for kc in range(NKC):
                    lhs = ring[s1][:, kc * 256 + mi * 128: kc * 256 + (mi + 1) * 128]
                    P.mm(pB[0][:, :], lhs,
                         c.xn_sb[:, kc * T: kc * T + 512],
                         start=(kc == 0), stop=(kc == NKC - 1),
                         wait=[t1, bank_free[2 if mi == 0 else 0]])
                    h1 = P.mm(pB[1][:, :], lhs,
                              c.xn_sb[:, kc * T + 512: kc * T + 1024],
                              start=(kc == 0), stop=(kc == NKC - 1),
                              wait=[bank_free[3 if mi == 0 else 1]],
                              inc=("pe", 1) if kc == NKC - 1 else None)
                for th in range(2):
                    iA = (0 if mi == 0 else 2) + th
                    iB = (2 if mi == 0 else 0) + th
                    sidx = th
                    sv = P.op("scalar", "activation", c.s_buf[sidx][:, :],
                              pA[th][:, :], AF.Silu, wait=[h0, c.s_free[sidx]],
                              inc=("act", 1))
                    bank_free[iA] = sv
                    hg = P.op("vector", "tensor_tensor", c.hg_buf[sidx][:, :],
                              pB[th][:, :],
                              c.gb_sb[gbi][:, th * 512:(th + 1) * 512], ALU.mult,
                              wait=[h1, c.hg_free[sidx]] + gb_ready, inc=("vec", 1))
                    bank_free[iB] = hg
                    it = P.op("vector", "tensor_tensor",
                              c.inter_sb[:, m * 1024 + th * 512: m * 1024 + (th + 1) * 512],
                              c.s_buf[sidx][:, :], c.hg_buf[sidx][:, :], ALU.mult,
                              wait=[sv, hg], inc=("vec", 1))
                    c.s_free[sidx] = it
                    c.hg_free[sidx] = it
                    inter_toks.append(it)
            pe_now = ("pe", P.cnt["pe"])
            ring_free[s0] = pe_now
            ring_free[s1] = pe_now
            if mq == 1:
                p_wi[(0, 2)] = load_piece(c.wi0p[e, 2])
                p_wi[(1, 2)] = load_piece(c.wi1p[e, 2])
            elif mq == 2:
                p_wi[(0, 3)] = load_piece(c.wi0p[e, 3])
                p_wi[(1, 3)] = load_piece(c.wi1p[e, 3])
            elif mq == 3:
                wo_pieces.append(load_piece(c.wop[e, 0]))
                wo_pieces.append(load_piece(c.wop[e, 1]))

        inter_done = inter_toks[-1]
        for dq in range(4):
            sw, tw = wo_pieces[dq]
            for dm in range(4):
                dmg = dq * 4 + dm
                pw = [banks[4], banks[5]] if dm % 2 == 0 else [banks[6], banks[7]]
                bi = 4 if dm % 2 == 0 else 6
                for kc in range(8):
                    lhs = ring[sw][:, kc * 512 + dm * 128: kc * 512 + (dm + 1) * 128]
                    P.mm(pw[0][:, :], lhs,
                         c.inter_sb[:, kc * 1024: kc * 1024 + 512],
                         start=(kc == 0), stop=(kc == 7),
                         wait=[tw, inter_done, bank_free[bi]])
                    wo = P.mm(pw[1][:, :], lhs,
                              c.inter_sb[:, kc * 1024 + 512: kc * 1024 + 1024],
                              start=(kc == 0), stop=(kc == 7),
                              wait=[bank_free[bi + 1]],
                              inc=("pe", 1) if kc == 7 else None)
                for th in range(2):
                    sl = slice(dmg * T + th * 512, dmg * T + (th + 1) * 512)
                    stt = P.op("vector", "scalar_tensor_tensor", x_sb[:, sl],
                               pw[th][:, :], 0.5, x_sb[:, sl], ALU.mult, ALU.add,
                               wait=[wo, st_prev.get((dmg, th))], inc=("vec", 1))
                    st_prev[(dmg, th)] = stt
                    bank_free[bi + th] = stt
            ring_free[sw] = ("pe", P.cnt["pe"])
            if dq == 0:
                wo_pieces.append(load_piece(c.wop[e, 2]))
            elif dq == 1:
                wo_pieces.append(load_piece(c.wop[e, 3]))
    c.bank_guard = dict(bank_free)
    return [st_prev[(g2, th)] for g2 in range(16) for th in range(2)]





def build_delta_trig(c):
    P = c.P
    d = c.dt_sb
    di = c.dti_sb
    ld = P.dma("sync", "ddl", d[0:32, 0:1], c.delta)
    t0 = P.op("vector", "tensor_scalar", d[0:32, 1:2], d[0:32, 0:1],
              1.0 / TWO_PI, None, ALU.mult, wait=[ld], inc=("vec", 1))
    t1 = P.op("vector", "tensor_copy", di[0:32, 0:1], d[0:32, 1:2], wait=[t0],
              inc=("vec", 1))
    t2 = P.op("vector", "tensor_copy", d[0:32, 2:3], di[0:32, 0:1], wait=[t1],
              inc=("vec", 1))
    t3 = P.op("vector", "scalar_tensor_tensor", d[0:32, 3:4], d[0:32, 2:3],
              -TWO_PI, d[0:32, 0:1], ALU.mult, ALU.add, wait=[t2], inc=("vec", 1))
    s0 = P.op("scalar", "activation", d[0:32, 4:5], d[0:32, 3:4], AF.Sin,
              wait=[t3], inc=("act", 1))
    s1 = P.op("scalar", "activation", d[0:32, 5:6], d[0:32, 3:4], AF.Sin,
              scale=0.5, wait=[t3], inc=("act", 1))
    s2 = P.op("scalar", "activation", d[0:32, 6:7], d[0:32, 5:6], AF.Square,
              wait=[s1, t3], inc=("act", 1))
    t4 = P.op("vector", "tensor_scalar", d[0:32, 7:8], d[0:32, 6:7], -2.0, 1.0,
              ALU.mult, ALU.add, wait=[s2], inc=("vec", 1))
    c.sin_d = d[0:32, 4:5]
    c.cos_d = d[0:32, 7:8]
    return [s0, t4]


def build_angle_tables(c, banks):
    P = c.P
    sc = c.pope_scr
    t1 = sc[0:32, 0:512]
    t2 = sc[0:32, 512:1024]
    t3 = sc[0:32, 1024:1536]
    t4 = sc[0:32, 1536:2048]
    toks = []
    prev = None
    for tb in range(16):
        sl = slice(tb * 512, (tb + 1) * 512)
        la = P.dma("gpsimd", "ang0", c.angb[0][0:32, 0:512], c.cosA[0:32, sl],
                   wait=[prev])
        lb = P.dma("gpsimd", "ang0", c.angb[0][0:32, 512:1024], c.sinA[0:32, sl])
        w0 = [la, lb, prev] + c.trig_toks
        a0 = P.op("vector", "tensor_scalar", t1, c.angb[0][0:32, 0:512], c.cos_d,
                  None, ALU.mult, wait=w0, inc=("vec", 1))
        a1 = P.op("vector", "tensor_scalar", t2, c.angb[0][0:32, 512:1024],
                  c.sin_d, None, ALU.mult, inc=("vec", 1))
        a2 = P.op("vector", "tensor_tensor", c.cosAng[0:32, sl], t1, t2,
                  ALU.subtract, wait=[a0, a1], inc=("vec", 1))
        a3 = P.op("vector", "tensor_scalar", t3, c.angb[0][0:32, 0:512], c.sin_d,
                  None, ALU.mult, inc=("vec", 1))
        a4 = P.op("vector", "tensor_scalar", t4, c.angb[0][0:32, 512:1024],
                  c.cos_d, None, ALU.mult, inc=("vec", 1))
        a5 = P.op("vector", "tensor_tensor", c.sinAng[0:32, sl], t3, t4,
                  ALU.add, wait=[a3, a4], inc=("vec", 1))
        prev = a5
        toks.append(a5)
    c.ang_done = toks[-1]
    return toks


def build_pope_chunk(c, raw1, raw2, pack, tb, wait):
    P = c.P
    pi = c.pope_i[0] % 2
    c.pope_i[0] += 1
    po = pi * 2048
    sl_tok = slice(tb * 512, (tb + 1) * 512)
    sc = c.pope_scr
    t1 = sc[0:32, pi * 1024: pi * 1024 + 512]
    t2 = sc[0:32, pi * 1024 + 512: pi * 1024 + 1024]
    sb = c.pope_bf
    mu1 = sb[0:32, po: po + 512]
    mu2 = sb[0:32, po + 512: po + 1024]
    rt1 = sb[0:32, po + 1024: po + 1536]
    rt2 = sb[0:32, po + 1536: po + 2048]
    o2 = c.rp_stage[0:32, pi * 512:(pi + 1) * 512]
    mus = []
    for i_, (raw, mu) in enumerate(((raw1, mu1), (raw2, mu2))):
        tr = rt1 if i_ == 0 else rt2
        b0 = P.op("scalar", "activation", t1, raw, AF.Abs,
                  wait=list(wait) + mus + [c.pope_free[pi]], inc=("act", 1))
        b1 = P.op("scalar", "activation", t2, t1, AF.Exp, scale=-1.0,
                  wait=[b0], inc=("act", 1))
        b2 = P.op("scalar", "activation", t1, t2, AF.Ln, bias=1.0,
                  wait=[b1], inc=("act", 1))
        b3 = P.op("vector", "tensor_scalar", tr, raw, 0.0, None, ALU.max,
                  wait=list(wait) + [c.pope_free[pi]], inc=("vec", 1))
        mus.append(P.op("vector", "tensor_tensor", mu, t1, tr, ALU.add,
                        wait=[b2, b3], inc=("vec", 1)))
    w0 = mus + [c.ang_done]
    r0 = P.op("vector", "tensor_tensor", rt1, mu1, c.cosAng[0:32, sl_tok],
              ALU.mult, wait=w0, inc=("vec", 1))
    r1 = P.op("vector", "tensor_tensor", rt2, mu2, c.sinAng[0:32, sl_tok],
              ALU.mult, wait=[r0], inc=("vec", 1))
    x1 = P.op("vector", "tensor_tensor", pack[0:32, sl_tok], rt1, rt2,
              ALU.subtract, wait=[r0, r1, c.pack_ms], inc=("vec", 1))
    r2 = P.op("vector", "tensor_tensor", rt1, mu1, c.sinAng[0:32, sl_tok],
              ALU.mult, wait=[x1], inc=("vec", 1))
    r3 = P.op("vector", "tensor_tensor", rt2, mu2, c.cosAng[0:32, sl_tok],
              ALU.mult, wait=[r2], inc=("vec", 1))
    x2 = P.op("vector", "tensor_tensor", o2, rt1, rt2, ALU.add,
              wait=[r2, r3, c.rp_free[pi]], inc=("vec", 1))
    dm = P.dma("scalar", f"rp{pi}", pack[32:64, sl_tok], o2,
               wait=[x2, c.pack_ms])
    c.rp_free[pi] = dm
    c.pope_free[pi] = x2
    return dm


def build_proj_only(c, xn_toks, banks):
    P = c.P
    wtok = c.wat_tok
    bank_free = {}

    def proj(out_sb, w_sb, nm):
        toks = {}
        for m in range(nm):
            for th in range(2):
                par = (m * 2 + th) % 4
                bk = banks[par]
                for kc in range(NKC):
                    mm = P.mm(bk[:, :],
                              w_sb[:, kc * (nm * 128) + m * 128:
                                   kc * (nm * 128) + (m + 1) * 128],
                              c.xn_sb[:, kc * T + th * 512: kc * T + (th + 1) * 512],
                              start=(kc == 0), stop=(kc == NKC - 1),
                              wait=[wtok, xn_toks[kc], bank_free.get(par)],
                              inc=("pe", 1) if kc == NKC - 1 else None)
                ev = P.op("scalar", "activation",
                          out_sb[:, m * T + th * 512: m * T + (th + 1) * 512],
                          bk[:, :], AF.Copy, wait=[mm], inc=("act", 1))
                bank_free[par] = ev
                toks[(m, th)] = ev
        return toks

    ckv_toks = proj(c.ckv_sb, c.wdkv_sb, 4)
    kr_toks = []
    for th in range(2):
        bk = banks[4 + th]
        for kc in range(NKC):
            mm = P.mm(bk[0:64, :],
                      c.wkr_sb[:, kc * 64: (kc + 1) * 64],
                      c.xn_sb[:, kc * T + th * 512: kc * T + (th + 1) * 512],
                      start=(kc == 0), stop=(kc == NKC - 1),
                      wait=[wtok, xn_toks[kc]],
                      inc=("pe", 1) if kc == NKC - 1 else None)
        kr_toks.append(P.op("scalar", "activation",
                            c.kr_sb[0:64, th * 512:(th + 1) * 512], bk[0:64, :],
                            AF.Copy, wait=[mm], inc=("act", 1)))
    wa = P.dma("sync", "agw1",
               c.ag1i[1024:1536].rearrange("(c p) t -> p c t", p=128),
               c.ckv_sb.ap().rearrange("p (c t) -> p c t", c=4),
               wait=list(ckv_toks.values()))
    wk = P.dma("sync", "agw1", c.ag1i[1536:1600], c.kr_sb[0:64, :],
               wait=kr_toks)
    cq_toks = proj(c.cq_sb, c.wdq_sb, 8)
    wq = P.dma("sync", "agw1",
               c.ag1i[0:1024].rearrange("(c p) t -> p c t", p=128),
               c.cq_sb.ap().rearrange("p (c t) -> p c t", c=8),
               wait=list(cq_toks.values()))
    return (wa, wk, wq)


def build_kr_tb(c, banks, tb, prevs):
    P = c.P
    g = tb // 2
    toff = (tb % 2) * 512
    kb = (tb % 2) * 1024
    lk1 = P.dma("sync", f"krs{tb % 2}", c.krb0[0:32, kb:kb + 512],
                c.ag1o[g * 1600 + 1536: g * 1600 + 1568, toff:toff + 512],
                wait=[c.cc2_tok, prevs[tb % 2]])
    lk2 = P.dma("sync", f"krs{tb % 2}", c.krb0[0:32, kb + 512:kb + 1024],
                c.ag1o[g * 1600 + 1568: g * 1600 + 1600, toff:toff + 512])
    tok = build_pope_chunk(c, c.krb0[0:32, kb:kb + 512],
                           c.krb0[0:32, kb + 512:kb + 1024],
                           c.krp, tb, [lk1, lk2])
    prevs[tb % 2] = tok
    c.kr_toks[tb] = tok


def _emit_pf(c, ptb):
    P = c.P
    ppb = ptb % 2
    pg = ptb // 2
    ptoff = (ptb % 2) * 512
    if ptb < 2:
        ckdst = c.ckb[ppb][:, :]
        cqdst = c.cqb[ppb][:, :]
    else:
        ckdst = c.ckb3[:, (ptb - 2) * 2048:(ptb - 1) * 2048]
        cqdst = c.cqb3[:, (ptb - 2) * 4096:(ptb - 1) * 4096]
    lk = P.dma("sync", f"pfk{ppb}", ckdst,
               c.ag1o[pg * 1600 + 1024: pg * 1600 + 1536]
               .rearrange("(c p) t -> p c t", p=128)[:, :, ptoff:ptoff + 512],
               wait=[c.cc1_tok, c.h_build_done])
    lq = P.dma("scalar", f"pfq{ppb}", cqdst,
               c.ag1o[pg * 1600: pg * 1600 + 1024]
               .rearrange("(c p) t -> p c t", p=128)[:, :, ptoff:ptoff + 512],
               wait=[c.cc1_tok, c.h_build_done])
    c.pf_toks[ptb] = (lk, lq)


def build_head(c, hh, banks, head_guard):
    P = c.P
    wat2 = c.wat2_tok
    kc_tok = {}
    v_tok = {}
    q_tok = {}
    qr_tok = {}
    if hh == 0:
        c.vmemset = P.op("vector", "memset", c.vaug_sb[:, :], 1.0, inc=("vec", 1))
    c.qn_toks = []
    c.kn_toks = []
    pend_pope = None
    prev_build = c.h_build_done if hh == 1 else None
    sbk = [prev_build, prev_build]
    for tb in range(16):
        pb = tb % 2
        g = tb // 2
        toff = (tb % 2) * 512
        ckS, cko = c.ckb[pb], 0
        cqS, cqo = c.cqb[pb], 0
        if hh == 1 and tb < 4:
            lck, lcq = c.pf_toks[tb]
            if tb >= 2:
                ckS, cko = c.ckb3, (tb - 2) * 2048
                cqS, cqo = c.cqb3, (tb - 2) * 4096
        else:
            lck = P.dma("gpsimd", f"ck{pb}", c.ckb[pb][:, :],
                        c.ag1o[g * 1600 + 1024: g * 1600 + 1536]
                        .rearrange("(c p) t -> p c t", p=128)[:, :, toff:toff + 512],
                        wait=[c.cc1_tok, sbk[pb]])
            lcq = P.dma("gpsimd", f"cq{pb}", c.cqb[pb][:, :],
                        c.ag1o[g * 1600: g * 1600 + 1024]
                        .rearrange("(c p) t -> p c t", p=128)[:, :, toff:toff + 512],
                        wait=[c.cc1_tok, sbk[pb]])
        bk = banks[pb]
        for kc in range(4):
            mm = P.mm(bk[:, :], c.wuk_sb[hh][:, kc * 128:(kc + 1) * 128],
                      ckS[:, cko + kc * 512: cko + (kc + 1) * 512],
                      start=(kc == 0), stop=(kc == 3),
                      wait=[wat2, lck, c.kcfree[pb], head_guard],
                      inc=("pe", 1) if kc == 3 else None)
        kc_tok[tb] = P.op("vector", "tensor_copy",
                          c.kcbuf[:, tb * 512:(tb + 1) * 512], bk[:, :],
                          wait=[mm], inc=("vec", 1))
        c.kcfree[pb] = kc_tok[tb]
        bv = banks[2 + pb]
        for ts in range(4):
            j = tb * 4 + ts
            for kc in range(4):
                mm = P.mm(bv[:, ts * 128:(ts + 1) * 128],
                          ckS[:, cko + kc * 512 + ts * 128:
                              cko + kc * 512 + (ts + 1) * 128],
                          c.wuv_sb[hh][:, kc * 128:(kc + 1) * 128],
                          start=(kc == 0), stop=(kc == 3),
                          wait=[wat2, lck, c.vfree[pb], head_guard, c.vmemset],
                          inc=("pe", 1) if (ts == 3 and kc == 3) else None)
        vev = P.op("vector", "tensor_copy",
                   c.vaug_sb.ap()
                   .rearrange("p (j d) -> p j d", d=129)[:, tb * 4:(tb + 1) * 4, 0:128],
                   bv[:, 0:512].rearrange("p (j d) -> p j d", d=128),
                   wait=[mm], inc=("vec", 1))
        for ts in range(4):
            v_tok[tb * 4 + ts] = vev
        c.vfree[pb] = vev
        bq = banks[4 + pb]
        for kc in range(8):
            mm = P.mm(bq[:, :], c.wuq_sb[hh][:, kc * 128:(kc + 1) * 128],
                      cqS[:, cqo + kc * 512: cqo + (kc + 1) * 512],
                      start=(kc == 0), stop=(kc == 7),
                      wait=[wat2, lcq, c.qcfree[pb], head_guard],
                      inc=("pe", 1) if kc == 7 else None)
        q_tok[tb] = P.op("vector", "tensor_copy",
                         c.qcbuf[:, tb * 512:(tb + 1) * 512], bq[:, :],
                         wait=[mm], inc=("vec", 1))
        c.qcfree[pb] = q_tok[tb]
        bg = banks[6]
        bq2 = banks[4 + (1 - pb)]
        for kc in range(8):
            mm1 = P.mm(bg[0:32, :], c.wqr_sb[hh][:, kc * 64: kc * 64 + 32],
                       cqS[:, cqo + kc * 512: cqo + (kc + 1) * 512],
                       start=(kc == 0), stop=(kc == 7),
                       wait=[wat2, lcq, c.gfree, head_guard],
                       inc=("pe", 1) if kc == 7 else None)
        for kc in range(8):
            mm2 = P.mm(bq2[0:32, :], c.wqr_sb[hh][:, kc * 64 + 32: kc * 64 + 64],
                       cqS[:, cqo + kc * 512: cqo + (kc + 1) * 512],
                       start=(kc == 0), stop=(kc == 7),
                       wait=[wat2, lcq, c.qcfree[1 - pb], head_guard],
                       inc=("pe", 1) if kc == 7 else None)
        qb_ = pb * 1024
        rw1 = P.op("vector", "tensor_copy", c.qraw[0:32, qb_:qb_ + 512],
                   bg[0:32, :], wait=[mm1, c.qrawfree[pb]], inc=("vec", 1))
        c.gfree = rw1
        rw2 = P.op("vector", "tensor_copy", c.qraw[0:32, qb_ + 512:qb_ + 1024],
                   bq2[0:32, :], wait=[mm2], inc=("vec", 1))
        c.qcfree[1 - pb] = rw2
        if pend_pope is not None:
            ptb, prw1, prw2, ppb = pend_pope
            pqb = ppb * 1024
            qr_tok[ptb] = build_pope_chunk(c, c.qraw[0:32, pqb:pqb + 512],
                                           c.qraw[0:32, pqb + 512:pqb + 1024],
                                           c.qrp, ptb, [prw1, prw2])
            c.qrawfree[ppb] = qr_tok[ptb]
            if hh == 0:
                build_kr_tb(c, banks, ptb, c.kr_prevs)
        pend_pope = (tb, rw1, rw2, pb)
        sbk[pb] = ("pe", P.cnt["pe"])
    c.h_build_done = ("pe", P.cnt["pe"])
    if hh == 0:
        c.pf_toks = {}
        for ptb in range(2):
            _emit_pf(c, ptb)
    ptb, prw1, prw2, ppb = pend_pope
    pqb = ppb * 1024
    qr_tok[ptb] = build_pope_chunk(c, c.qraw[0:32, pqb:pqb + 512],
                                   c.qraw[0:32, pqb + 512:pqb + 1024],
                                   c.qrp, ptb, [prw1, prw2])
    c.qrawfree[ppb] = qr_tok[ptb]
    if hh == 0:
        build_kr_tb(c, banks, ptb, c.kr_prevs)
    build_done = [kc_tok[15], q_tok[15], qr_tok[15], v_tok[63]]

    sq_free = [None, None]
    bank_n = [banks[0], banks[6]]
    bank_n_free = [c.scfree[0], c.gfree]
    for i, tc in enumerate(range(16)):
        slq = slice(tc * 512, (tc + 1) * 512)
        pi2 = i % 2
        bn = bank_n[pi2]
        sq1 = P.op("scalar", "activation", c.sqn[:, 0:512], c.qcbuf[:, slq],
                   AF.Square, wait=[q_tok[tc], sq_free[0]], inc=("act", 1))
        sq2 = P.op("scalar", "activation", c.sqr[0:64, 0:512],
                   c.qrp[0:64, slq], AF.Square, wait=[qr_tok[tc], sq_free[1]],
                   inc=("act", 1))
        sq3 = P.op("scalar", "activation", c.sqk[:, 0:512], c.kcbuf[:, slq],
                   AF.Square, wait=[kc_tok[tc], sq_free[0], sq_free[1]],
                   inc=("act", 1))
        mm_ = P.mm(bn[0:1, 0:512], c.onesb[:, 0:1], c.sqn[:, 0:512],
                   start=True, stop=False, wait=[sq1, bank_n_free[pi2]])
        mm_ = P.mm(bn[0:1, 0:512], c.onesb[0:64, 0:1], c.sqr[0:64, 0:512],
                   start=False, stop=True, wait=[sq2], inc=("pe", 1))
        rm = P.op("vector", "reduce_max", c.mxp[0:1, tc: tc + 1],
                  bn[0:1, 0:512], axis=AX, wait=[mm_], inc=("vec", 1))
        c.qn_toks.append(rm)
        mmk = P.mm(bn[32:33, 0:512], c.onesb[:, 0:1], c.sqk[:, 0:512],
                   start=True, stop=True, wait=[sq3, rm], inc=("pe", 1))
        sq_free[0] = mmk
        rmk = P.op("vector", "reduce_max", c.mxp[0:1, 16 + tc: 17 + tc],
                   bn[32:33, 0:512], axis=AX, wait=[mmk], inc=("vec", 1))
        c.kn_toks.append(rmk)
        bank_n_free[pi2] = rmk
        sq_free[1] = mmk
        if hh == 0:
            sqv = P.op("scalar", "activation", c.sqk[0:64, 0:512],
                       c.krp[0:64, slq], AF.Square,
                       wait=[c.kr_toks[tc], rmk], inc=("act", 1))
            mmv = P.mm(bn[64:65, 0:512], c.onesb[0:64, 0:1],
                       c.sqk[0:64, 0:512], start=True, stop=True,
                       wait=[sqv, rmk], inc=("pe", 1))
            rmv = P.op("vector", "reduce_max", c.mxp[0:1, 32 + tc: 33 + tc],
                       bn[64:65, 0:512], axis=AX, wait=[mmv], inc=("vec", 1))
            bank_n_free[pi2] = rmv
            sq_free[0] = mmv
            c.kr_nmax.append(rmv)
    if hh == 0:
        c.krmax_tok = P.op("vector", "reduce_max", c.krmax[0:1, 0:1],
                           c.mxp[0:1, 32:48], axis=AX, wait=c.kr_nmax,
                           inc=("vec", 1))
    c.gfree = bank_n_free[1]
    c.scfree[0] = bank_n_free[0]

    m0 = P.op("vector", "reduce_max", c.mxq[0:1, 0:1], c.mxp[0:1, 0:16], axis=AX,
              wait=c.qn_toks, inc=("vec", 1))
    m1 = P.op("vector", "reduce_max", c.mxq[0:1, 1:2], c.mxp[0:1, 16:32], axis=AX,
              wait=c.kn_toks, inc=("vec", 1))
    m2 = P.op("vector", "tensor_tensor", c.mxq[0:1, 2:3], c.mxq[0:1, 1:2],
              c.krmax[0:1, 0:1], ALU.add, wait=[m1, c.krmax_tok], inc=("vec", 1))
    m3 = P.op("vector", "tensor_tensor", c.mxq[0:1, 3:4], c.mxq[0:1, 0:1],
              c.mxq[0:1, 2:3], ALU.mult, wait=[m0, m2], inc=("vec", 1))
    l0 = P.op("scalar", "activation", c.mxq[0:1, 4:5], c.mxq[0:1, 3:4], AF.Ln,
              wait=[m3], inc=("act", 1))
    l1 = P.op("scalar", "activation", c.mxq[0:1, 5:6], c.mxq[0:1, 4:5], AF.Exp,
              scale=0.5, bias=c.clog_sb[0:1, 0:1], wait=[l0, c.clog_tok],
              inc=("act", 1))
    l2 = P.op("vector", "tensor_scalar", c.mxq[0:1, 6:7], c.mxq[0:1, 5:6], -1.0,
              None, ALU.mult, wait=[l1], inc=("vec", 1))
    bmm = P.mm(banks[6][:, 0:1], c.ones2d_sb[0:1, 0:128], c.mxq[0:1, 6:7],
               start=True, stop=True, wait=[l2, c.gfree], inc=("pe", 1))
    negc = P.op("vector", "tensor_copy", c.negc_sb[:, 0:1], banks[6][:, 0:1],
                wait=[bmm], inc=("vec", 1))
    c.gfree = negc
    if hh == 0:
        for ptb in (2, 3):
            _emit_pf(c, ptb)

    probs_free = c.probs_free
    av_free = c.av_free
    scfree = c.scfree
    out_toks = []
    for s in range(16):
        avstart = {t: True for t in range(4)}
        pend = None

        def emit_av(jv, eev):
            nonlocal_last = None
            for t in range(max(0, jv - 4 * s), 4):
                ba = c.avbanks[t]
                tk = P.mm(ba[:, 0:129], c.probs[jv % 2][:, t * 128:(t + 1) * 128],
                          c.vaug_sb[:, jv * 129:(jv + 1) * 129],
                          start=avstart[t], stop=(jv == 4 * s + t),
                          wait=[eev, v_tok[jv], av_free[t]],
                          inc=("pe", 1) if t == 3 else None)
                avstart[t] = False
            return tk

        for j in range(4 * s + 4):
            pb = j % 2
            bsc = banks[pb]
            mm1 = P.mm(bsc[:, :], c.kcbuf[:, j * 128:(j + 1) * 128],
                       c.qcbuf[:, s * 512:(s + 1) * 512], start=True, stop=False,
                       wait=[kc_tok[j // 4], q_tok[s], scfree[pb], negc,
                             build_done[0]])
            mm2 = P.mm(bsc[:, :], c.krp[:, j * 128:(j + 1) * 128],
                       c.qrp[:, s * 512:(s + 1) * 512], start=False,
                       stop=True, wait=[c.kr_toks[j // 4], qr_tok[s]],
                       inc=("pe", 1))
            mtok = mm2
            if 0 <= j - 4 * s <= 3:
                t = j - 4 * s
                mtok = P.op("vector", "tensor_tensor",
                            bsc[:, t * 128:(t + 1) * 128],
                            bsc[:, t * 128:(t + 1) * 128], c.maskT_sb[:, :],
                            ALU.add, wait=[mm2], inc=("vec", 1))
            ee = P.op("scalar", "activation", c.probs[pb][:, :], bsc[:, :],
                      AF.Exp, scale=SCALE, bias=c.negc_sb[:, 0:1],
                      wait=[mtok, probs_free[pb], negc], inc=("act", 1))
            scfree[pb] = ee
            if pend is not None:
                probs_free[pend[0] % 2] = emit_av(*pend)
            pend = (j, ee)
        probs_free[pend[0] % 2] = emit_av(*pend)
        ot = []
        for t in range(4):
            ba = c.avbanks[t]
            dn = P.op("vector", "tensor_scalar", c.rcol[:, t: t + 1],
                      ba[:, 128:129], 1e-30, None, ALU.max,
                      wait=[probs_free[0], probs_free[1]], inc=("vec", 1))
            rc = P.op("vector", "reciprocal", c.rcol[:, 4 + t: 5 + t],
                      c.rcol[:, t: t + 1], wait=[dn], inc=("vec", 1))
            sc = P.op("vector", "tensor_scalar", c.oav[:, t * 128:(t + 1) * 128],
                      ba[:, 0:128], c.rcol[:, 4 + t: 5 + t], None, ALU.mult,
                      wait=[rc, c.oavfree[t]], inc=("vec", 1))
            av_free[t] = sc
            tp = P.op("tensor", "transpose", c.pbf[:, t * 128:(t + 1) * 128],
                      c.oav[:, t * 128:(t + 1) * 128], c.identb_sb[:, :],
                      wait=[sc, c.pbffree], inc=("pe", 1))
            ot.append(P.op("scalar", "activation",
                           c.outT[:, t * 128:(t + 1) * 128],
                           c.pbf[:, t * 128:(t + 1) * 128], AF.Copy,
                           wait=[tp, c.outTfree], inc=("act", 1)))
            c.oavfree[t] = tp
        c.pbffree = ot[-1]
        rj = s // 2
        wtok2 = P.dma("sync", "agw2",
                      c.ag2i[rj * 256 + hh * 128: rj * 256 + (hh + 1) * 128,
                             (s % 2) * 512: (s % 2) * 512 + 512],
                      c.outT[:, :], wait=ot)
        c.outTfree = wtok2
        out_toks.append(wtok2)
    return out_toks, ("pe", P.cnt["pe"])


def build_outproj(c, x2_sb, banks):
    P = c.P
    ld_ag = []
    for kc in range(NKC):
        ld_ag.append(P.dma("sync", f"ao{kc % 2}",
                           c.agown_sb[:, kc * T:(kc + 1) * T],
                           c.ag2o.rearrange("(c p) t -> p c t", p=128)[:, kc, :],
                           wait=[c.cc3_tok]))
    xload = P.dma("sync", "spl", x2_sb[:, :], c.xspill, wait=[c.spill_tok])
    prev_piece = [None, None]
    st_prev = {}
    bank_free = {}
    for dmq in range(4):
        pp = dmq % 2
        lw = P.dma("gpsimd", f"wo2{pp}", c.wout_buf[pp][:, :], c.woutp[dmq],
                   wait=[prev_piece[pp]])
        for dsub in range(4):
            dmg = dmq * 4 + dsub
            for th in range(2):
                par = (dsub * 2 + th) % 2
                bk = banks[par]
                for kc in range(NKC):
                    mm = P.mm(bk[:, :],
                              c.wout_buf[pp][:, kc * 512 + dsub * 128:
                                             kc * 512 + (dsub + 1) * 128],
                              c.agown_sb[:, kc * T + th * 512:
                                         kc * T + (th + 1) * 512],
                              start=(kc == 0), stop=(kc == NKC - 1),
                              wait=[lw, ld_ag[kc], bank_free.get(par),
                                    c.p4_guard],
                              inc=("pe", 1) if kc == NKC - 1 else None)
                sl = slice(dmg * T + th * 512, dmg * T + (th + 1) * 512)
                stt = P.op("vector", "tensor_tensor", x2_sb[:, sl], bk[:, :],
                           x2_sb[:, sl], ALU.add,
                           wait=[mm, xload, st_prev.get((dmg, th))],
                           inc=("vec", 1))
                st_prev[(dmg, th)] = stt
                bank_free[par] = stt
        prev_piece[pp] = ("pe", P.cnt["pe"])
    return [st_prev[(g2, th)] for g2 in range(16) for th in range(2)]



def _alloc_ffn_bufs(c, en):
    nc = c.nc
    u = c._uniq
    c._uniq += 1
    c.ln_scratch2 = [en(nc.sbuf_tensor(f"lt{u}_{i}", [128, 1024], F32)) for i in range(2)]
    c.stat_rows = [en(nc.sbuf_tensor(f"str{u}_{i}", [1, 1024], F32)) for i in range(4)]
    c.nm_b = en(nc.sbuf_tensor(f"nm_b{u}", [128, 1024], F32))
    c.rs_b = en(nc.sbuf_tensor(f"rs_b{u}", [128, 1024], F32))
    c.gate_sb = en(nc.sbuf_tensor(f"gate_sb{u}", [128, 64], F32))
    c.gcol_sb = en(nc.sbuf_tensor(f"gcol_sb{u}", [128, 8], F32))
    c.geq_sb = en(nc.sbuf_tensor(f"geq_sb{u}", [128, 32], F32))
    c.gres_sb = en(nc.sbuf_tensor(f"gres_sb{u}", [128, 64], F32))
    c.gT_sb = en(nc.sbuf_tensor(f"gT_sb{u}", [8, 1024], F32))


def _alloc_expert_bufs(c, en):
    nc = c.nc
    u = c._uniq
    c._uniq += 1
    c.ring = [en(nc.sbuf_tensor(f"ring{u}_{i}", [128, 4096], BF16)) for i in range(4)]
    c.ring_free = [None] * 4
    c.ring_i = [0]
    c.gb_sb = [en(nc.sbuf_tensor(f"gb{u}_{i}", [128, 1024], F32)) for i in range(2)]
    c.inter_sb = en(nc.sbuf_tensor(f"inter_sb{u}", [128, 8192], BF16))
    c.s_buf = [en(nc.sbuf_tensor(f"sbf{u}_{i}", [128, 512], F32)) for i in range(2)]
    c.hg_buf = [en(nc.sbuf_tensor(f"hgb{u}_{i}", [128, 512], F32)) for i in range(2)]
    c.s_free = [None, None]
    c.hg_free = [None, None]


def build_program():
    nc = bass.Bass("TRN2", target_bir_lowering=False)
    c = Ctx()
    c.nc = nc
    c._uniq = 0

    def dri(name, shape, dt=F32):
        return nc.dram_tensor(name, shape, dt, kind="ExternalInput").ap()

    c.x_in = dri("x_fm", [128, NKC * T])
    c.wi0p = dri("wi0p", [E, 4, 128, 16 * 256])
    c.wi1p = dri("wi1p", [E, 4, 128, 16 * 256])
    c.wop = dri("wop", [E, 4, 128, 8 * 512])
    c.wg = dri("wgp", [128, NKC * 8])
    c.wdqp = dri("wdqp", [128, NKC * 1024])
    c.wdkvp = dri("wdkvp", [128, NKC * 512])
    c.wkrp = dri("wkrp", [128, NKC * 64])
    c.wukp = dri("wukp", [2, 128, 4 * 128])
    c.wuvp = dri("wuvp", [2, 128, 4 * 128])
    c.wuqp = dri("wuqp", [2, 128, 8 * 128])
    c.wqrp = dri("wqrp", [2, 128, 8 * 64])
    c.woutp = dri("woutp", [4, 128, 16 * 512])
    c.delta = dri("delta", [32, 1])
    c.cosA = dri("cosA", [32, S])
    c.sinA = dri("sinA", [32, S])
    c.maskT = dri("maskT", [128, 128])
    c.identity = dri("ident", [128, 128])
    c.identb = dri("identb", [128, 128])
    c.ones2d = dri("ones2d", [128, 128])
    c.basis = dri("basis", [8, 1024])
    c.out = nc.dram_tensor("out", [D, T], F32, kind="ExternalOutput").ap()

    c.xspill = nc.dram_tensor("xspill", [128, NKC * T], F32).ap()
    c.ag1i = nc.dram_tensor("ag1i", [1600, 1024], BF16).ap()
    c.ag1o = nc.dram_tensor("ag1o", [8 * 1600, 1024], BF16).ap()
    c.ag2i = nc.dram_tensor("ag2i", [2048, 1024], BF16).ap()
    c.ag2o = nc.dram_tensor("ag2o", [2048, 1024], BF16).ap()

    RG = [list(range(NCORE))]

    with ExitStack() as stack:
        P = Prog(nc, stack, dma_streams=[
            ("din", "batch"), ("dinb", "batch"), ("dout", "batch"),
            ("r0", "serial"), ("r1", "serial"), ("r2", "serial"), ("r3", "serial"),
            ("wat", "batch"), ("wat2", "batch"), ("ddl", "batch"),
            ("spl", "serial"), ("agw1", "batch"), ("agw2", "batch"),
            ("ang0", "serial"), ("ang1", "serial"),
            ("cq0", "serial"), ("cq1", "serial"),
            ("ck0", "serial"), ("ck1", "serial"),
            ("krs0", "serial"), ("krs1", "serial"),
            ("ao0", "serial"), ("ao1", "serial"),
            ("rp0", "serial"), ("rp1", "serial"),
            ("wo20", "serial"), ("wo21", "serial"),
            ("pfk0", "serial"), ("pfk1", "serial"),
            ("pfq0", "serial"), ("pfq1", "serial"),
            ("wat3", "batch"),
        ])
        c.P = P
        en = stack.enter_context

        c.ones2d_sb = en(nc.sbuf_tensor("ones2d_sb", [128, 128], F32))
        c.ident_sb = en(nc.sbuf_tensor("ident_sb", [128, 128], F32))
        c.identb_sb = en(nc.sbuf_tensor("identb_sb", [128, 128], BF16))
        c.maskT_sb = en(nc.sbuf_tensor("maskT_sb", [128, 128], F32))
        c.basis_sb = en(nc.sbuf_tensor("basis_sb", [8, 1024], F32))
        c.wg_sb = en(nc.sbuf_tensor("wg_sb", [128, NKC * 8], F32))
        c.eps_sb = en(nc.sbuf_tensor("eps_sb", [1, 8], F32))
        c.clog_sb = en(nc.sbuf_tensor("clog_sb", [1, 8], F32))
        c.dt_sb = en(nc.sbuf_tensor("dt_sb", [32, 8], F32))
        c.dti_sb = en(nc.sbuf_tensor("dti_sb", [32, 1], I32))

        P.dma("sync", "din", c.ones2d_sb[:, :], c.ones2d)
        P.dma("sync", "din", c.ident_sb[:, :], c.identity)
        P.dma("gpsimd", "dinb", c.identb_sb[:, :], c.identb)
        P.dma("sync", "din", c.maskT_sb[:, :], c.maskT)
        P.dma("sync", "din", c.basis_sb[:, :], c.basis)
        P.dma("sync", "din", c.wg_sb[:, :], c.wg)
        c.eps_tok = P.op("vector", "memset", c.eps_sb[0:1, :], EPS, inc=("vec", 1))
        c.clog_tok = P.op("vector", "memset", c.clog_sb[0:1, :], CLOG, inc=("vec", 1))
        ctot = ("din", P.cnt["din"])
        ctotb = ("dinb", P.cnt["dinb"])
        for _e in ENGS:
            P.op(_e, "_waits_only", wait=[ctot, ctotb])

        with ExitStack() as p12:
          c.xn_sb = p12.enter_context(nc.sbuf_tensor("xn_sb", [128, NKC * T], BF16))
          with ExitStack() as p1:
            e1 = p1.enter_context
            x_sb = e1(nc.sbuf_tensor("x_sb", [128, NKC * T], F32))
            _alloc_ffn_bufs(c, e1)
            banks = [e1(nc.psum_tensor(f"pb{i}", [128, 512], F32)) for i in range(8)]
            c.x_ready = P.dma("sync", "din", x_sb[:, :], c.x_in)
            c.bank_guard = {}
            with ExitStack() as pf:
                _alloc_expert_bufs(c, pf.enter_context)
                last_stt = build_ffn(c, x_sb, c.x_ready, banks)
            xr2 = ("vec", max(v for s_, v in last_stt))
            c.wdq_sb = p12.enter_context(
                nc.sbuf_tensor("wdq_sb", [128, NKC * 1024], BF16, side="right"))
            c.wdkv_sb = p12.enter_context(
                nc.sbuf_tensor("wdkv_sb", [128, NKC * 512], BF16, side="right"))
            c.wkr_sb = p12.enter_context(
                nc.sbuf_tensor("wkr_sb", [128, NKC * 64], BF16, side="right"))
            P.dma("gpsimd", "wat", c.wdq_sb[:, :], c.wdqp, wait=[xr2])
            P.dma("gpsimd", "wat", c.wdkv_sb[:, :], c.wdkvp, wait=[xr2])
            c.wat_tok = P.dma("gpsimd", "wat", c.wkr_sb[:, :], c.wkrp,
                              wait=[xr2])
            xn2_toks, _ = build_ln(c, x_sb, xr2, banks, with_gate=False)
            c.spill_tok = P.dma("sync", "spl", c.xspill, x_sb[:, :],
                                wait=[xn2_toks[NKC - 1], c.ln_stat_done])
          P.barrier()
          with ExitStack() as p2:
            e2 = p2.enter_context
            c.cq_sb = e2(nc.sbuf_tensor("cq_sb", [128, 8 * T], BF16))
            c.ckv_sb = e2(nc.sbuf_tensor("ckv_sb", [128, 4 * T], BF16))
            c.kr_sb = e2(nc.sbuf_tensor("kr_sb", [64, T], BF16))
            banks2 = [e2(nc.psum_tensor(f"pc{i}", [128, 512], F32)) for i in range(6)]
            agw_ckv, agw_kr, agw_cq = build_proj_only(c, xn2_toks, banks2)
        P.barrier()

        with ExitStack() as p3:
            e3 = p3.enter_context
            banks = [e3(nc.psum_tensor(f"qb{i}", [128, 512], F32)) for i in range(7)]
            c.pbf = e3(nc.psum_tensor("pbf", [128, 1024], BF16))
            c.avbanks = [banks[2], banks[3], banks[4], banks[5]]
            c.pope_scr = e3(nc.sbuf_tensor("pope_scr", [32, 2048], BF16))
            c.pope_bf = e3(nc.sbuf_tensor("pope_bf", [32, 4096], BF16))
            c.cosAng = e3(nc.sbuf_tensor("cosAng", [32, S], BF16))
            c.sinAng = e3(nc.sbuf_tensor("sinAng", [32, S], BF16))
            c.angbt = e3(nc.sbuf_tensor("angbt", [32, 1024], BF16))
            c.angb = [c.angbt[0:32, 0:1024], c.angbt[0:32, 0:1024]]
            c.kcbuf = e3(nc.sbuf_tensor("kcbuf", [128, S], BF16))
            c.krp = e3(nc.sbuf_tensor("krp", [128, S], BF16))
            c.qcbuf = e3(nc.sbuf_tensor("qcbuf", [128, S], BF16))
            c.qrp = e3(nc.sbuf_tensor("qrp", [128, S], BF16))
            c.rp_stage = e3(nc.sbuf_tensor("rp_stage", [32, 1024], BF16))
            c.vaug_sb = e3(nc.sbuf_tensor("vaug", [128, 64 * 129], BF16))
            c.cqb = [e3(nc.sbuf_tensor(f"cqb{i}", [128, 8 * 512], BF16)) for i in range(2)]
            c.ckb = [e3(nc.sbuf_tensor(f"ckb{i}", [128, 4 * 512], BF16)) for i in range(2)]
            c.krb0 = e3(nc.sbuf_tensor("krb0", [32, 2048], BF16))
            c.krb = [c.krb0, c.krb0]
            c.qraw = e3(nc.sbuf_tensor("qraw", [32, 2048], BF16))
            c.sqn = e3(nc.sbuf_tensor("sqn", [128, 512], BF16))
            c.sqk = e3(nc.sbuf_tensor("sqk", [128, 512], BF16))
            c.onesb = e3(nc.sbuf_tensor("onesb", [128, 8], BF16))
            c.sqr = e3(nc.sbuf_tensor("sqr", [64, 512], BF16))
            c.mxp = e3(nc.sbuf_tensor("mxp", [1, 48], F32))
            c.mxq = e3(nc.sbuf_tensor("mxq", [1, 8], F32))
            c.krmax = e3(nc.sbuf_tensor("krmax", [1, 1], F32))
            c.negc_sb = e3(nc.sbuf_tensor("negc", [128, 8], F32))
            c.probs = [e3(nc.sbuf_tensor(f"probs{i}", [128, 512], BF16)) for i in range(2)]
            c.oav = e3(nc.sbuf_tensor("oav", [128, 512], BF16))
            c.outT = e3(nc.sbuf_tensor("outT", [128, 512], BF16))
            c.rcol = e3(nc.sbuf_tensor("rcol", [128, 8], F32))
            wuk1 = e3(nc.sbuf_tensor("wuk", [128, 4 * 128], BF16))
            wuv1 = e3(nc.sbuf_tensor("wuv", [128, 4 * 128], BF16))
            wuq1 = e3(nc.sbuf_tensor("wuq", [128, 8 * 128], BF16))
            wqr1 = e3(nc.sbuf_tensor("wqr", [128, 8 * 64], BF16))
            wuk2 = e3(nc.sbuf_tensor("wuk2", [128, 4 * 128], BF16))
            wuv2 = e3(nc.sbuf_tensor("wuv2", [128, 4 * 128], BF16))
            wuq2 = e3(nc.sbuf_tensor("wuq2", [128, 8 * 128], BF16))
            wqr2 = e3(nc.sbuf_tensor("wqr2", [128, 8 * 64], BF16))
            c.wuk_sb = [wuk1, wuk2]
            c.wuv_sb = [wuv1, wuv2]
            c.wuq_sb = [wuq1, wuq2]
            c.wqr_sb = [wqr1, wqr2]
            c.cqb3 = e3(nc.sbuf_tensor("cqb3", [128, 2 * 8 * 512], BF16))
            c.ckb3 = e3(nc.sbuf_tensor("ckb3", [128, 2 * 4 * 512], BF16))

            P.barrier()
            P.dma("gpsimd", "wat2", c.wuk_sb[0][:, :], c.wukp[0])
            P.dma("gpsimd", "wat2", c.wuv_sb[0][:, :], c.wuvp[0])
            P.dma("gpsimd", "wat2", c.wuq_sb[0][:, :], c.wuqp[0])
            c.wat2_tok = P.dma("gpsimd", "wat2", c.wqr_sb[0][:, :], c.wqrp[0])
            P.dma("gpsimd", "wat3", c.wuk_sb[1][:, :], c.wukp[1])
            P.dma("gpsimd", "wat3", c.wuv_sb[1][:, :], c.wuvp[1])
            P.dma("gpsimd", "wat3", c.wuq_sb[1][:, :], c.wuqp[1])
            c.wat3_tok = P.dma("gpsimd", "wat3", c.wqr_sb[1][:, :], c.wqrp[1])
            c.cc1_tok = P.op("gpsimd", "collective_compute", "AllGather",
                             ALU.bypass, RG, ins=[c.ag1i], outs=[c.ag1o],
                             wait=[agw_ckv, agw_kr, agw_cq], inc=("cc", 1))
            c.cc1a_tok = c.cc1_tok
            c.cc2_tok = c.cc1_tok
            c.trig_toks = build_delta_trig(c)
            c.pope_free = None
            c.rp_free = [None, None]
            c.pope_i = [0]
            c.gfree = None
            c.sqn_free = None
            c.sqr_free = None
            c.sqk_free = None
            build_angle_tables(c, banks)
            c.onesb_tok = P.op("vector", "memset", c.onesb[:, :], 1.0,
                               inc=("vec", 1))
            msk1 = P.op("vector", "memset", c.krp[:, :], 0.0, inc=("vec", 1))
            msk2 = P.op("vector", "memset", c.qrp[:, :], 0.0, inc=("vec", 1))
            c.pack_ms = msk2
            c.pope_free = [c.ang_done, c.ang_done]
            c.kr_toks = {}
            c.kr_nmax = []
            c.kr_prevs = [None, None]
            c.kcfree = [None, None]
            c.vfree = [None, None]
            c.qcfree = [None, None]
            c.qrawfree = [None, None]
            c.scfree = [None, None]
            c.probs_free = [None, None]
            c.av_free = [None] * 4
            c.oavfree = [None] * 4
            c.pbffree = None
            c.outTfree = None
            hg0 = None
            all_out = []
            for hh in range(2):
                if hh == 1:
                    c.wat2_tok = c.wat3_tok
                out_toks, pe_end = build_head(c, hh, banks, hg0)
                hg0 = pe_end
                all_out += out_toks
            c.cc3_tok = P.op("gpsimd", "collective_compute", "AllToAll",
                             ALU.bypass, RG, ins=[c.ag2i], outs=[c.ag2o],
                             wait=[c.cc2_tok, ("agw2", P.cnt["agw2"])],
                             inc=("cc", 1))
            P.barrier()

        with ExitStack() as p4:
            e4 = p4.enter_context
            x2_sb = e4(nc.sbuf_tensor("x2_sb", [128, NKC * T], F32))
            banks = [e4(nc.psum_tensor(f"rb{i}", [128, 512], F32)) for i in range(8)]
            c.p4_guard = None
            with ExitStack() as po:
                eo = po.enter_context
                c.agown_sb = eo(nc.sbuf_tensor("agown", [128, NKC * T], BF16))
                c.wout_buf = [eo(nc.sbuf_tensor(f"woutb{i}", [128, 16 * 512], BF16))
                              for i in range(2)]
                op_stt = build_outproj(c, x2_sb, banks)
            P.barrier()
            _alloc_ffn_bufs(c, e4)
            c.xn_sb = e4(nc.sbuf_tensor("xn2_sb", [128, NKC * T], BF16))
            with ExitStack() as pf2:
                _alloc_expert_bufs(c, pf2.enter_context)
                c.bank_guard = {}
                xr4 = ("vec", max(v for s_, v in op_stt))
                last2 = build_ffn(c, x2_sb, xr4, banks)
            out_tok = P.dma("sync", "dout",
                            c.out.rearrange("(c p) t -> p c t", p=128),
                            x2_sb.ap().rearrange("p (c t) -> p c t", c=NKC),
                            wait=last2)
            P.op("sync", "_waits_only", wait=[out_tok])

        with nc.Block() as block:
            P.replay(block)
    return nc



def prep_inputs(L0_attn_x, L0_attn_w_dkv, L0_attn_w_uk, L0_attn_w_uv,
                L0_attn_w_dq, L0_attn_w_uq, L0_attn_w_qr, L0_attn_w_kr,
                L0_attn_w_out, L0_attn_delta, L0_ffn_w_gate, L0_ffn_wi_0,
                L0_ffn_wi_1, L0_ffn_wo):
    cosA, sinA, maskT, ident, identb, ones2d, basis = make_consts()
    x = np.asarray(L0_attn_x)[0]
    xT = np.ascontiguousarray(x.T)
    wi0 = np.asarray(L0_ffn_wi_0)
    wi1 = np.asarray(L0_ffn_wi_1)
    wo = np.asarray(L0_ffn_wo)
    wi0p = np.stack([np.stack([chunk_pf(wi0[e][:, mq * 256:(mq + 1) * 256])
                               for mq in range(4)]) for e in range(E)])
    wi1p = np.stack([np.stack([chunk_pf(wi1[e][:, mq * 256:(mq + 1) * 256])
                               for mq in range(4)]) for e in range(E)])
    wop = np.stack([np.stack([chunk_pf(wo[e][:, dq * 512:(dq + 1) * 512])
                              for dq in range(4)]) for e in range(E)])
    wgp = chunk_pf(np.asarray(L0_ffn_w_gate))
    wdqp = chunk_pf(np.asarray(L0_attn_w_dq))
    wdkvp = chunk_pf(np.asarray(L0_attn_w_dkv))
    wkrp = chunk_pf(np.asarray(L0_attn_w_kr))
    w_uk = np.asarray(L0_attn_w_uk)
    w_uv = np.asarray(L0_attn_w_uv)
    w_uq = np.asarray(L0_attn_w_uq)
    w_qr = np.asarray(L0_attn_w_qr)
    woutp = np.stack([chunk_pf(np.asarray(L0_attn_w_out)[:, dmq * 512:(dmq + 1) * 512])
                      for dmq in range(4)])
    delta = np.ascontiguousarray(np.asarray(L0_attn_delta).reshape(32, 1))
    shared = {"wi0p": wi0p, "wi1p": wi1p, "wop": wop, "wgp": wgp,
              "wdqp": wdqp, "wdkvp": wdkvp, "wkrp": wkrp, "woutp": woutp,
              "delta": delta, "cosA": cosA, "sinA": sinA, "maskT": maskT,
              "ident": ident, "identb": identb, "ones2d": ones2d, "basis": basis}
    in_maps = []
    for cid in range(NCORE):
        xs = np.ascontiguousarray(xT[:, cid * T:(cid + 1) * T])
        m = dict(shared)
        m["x_fm"] = chunk_pf(xs)
        m["wukp"] = np.stack([chunk_pf(w_uk[:, (2 * cid + hh) * 128:
                                            (2 * cid + hh + 1) * 128])
                              for hh in range(2)])
        m["wuvp"] = np.stack([chunk_pf(w_uv[:, (2 * cid + hh) * 128:
                                            (2 * cid + hh + 1) * 128])
                              for hh in range(2)])
        m["wuqp"] = np.stack([chunk_pf(w_uq[:, (2 * cid + hh) * 128:
                                            (2 * cid + hh + 1) * 128])
                              for hh in range(2)])
        m["wqrp"] = np.stack([chunk_pf(w_qr[:, (2 * cid + hh) * 64:
                                            (2 * cid + hh + 1) * 64])
                              for hh in range(2)])
        in_maps.append(m)
    return in_maps


def kernel(L0_attn_x, L0_attn_w_dkv, L0_attn_w_uk, L0_attn_w_uv, L0_attn_w_dq,
           L0_attn_w_uq, L0_attn_w_qr, L0_attn_w_kr, L0_attn_w_out,
           L0_attn_delta, L0_ffn_w_gate, L0_ffn_wi_0, L0_ffn_wi_1, L0_ffn_wo):
    in_maps = prep_inputs(L0_attn_x, L0_attn_w_dkv, L0_attn_w_uk, L0_attn_w_uv,
                          L0_attn_w_dq, L0_attn_w_uq, L0_attn_w_qr, L0_attn_w_kr,
                          L0_attn_w_out, L0_attn_delta, L0_ffn_w_gate,
                          L0_ffn_wi_0, L0_ffn_wi_1, L0_ffn_wo)
    nc = build_program()
    res = run_bass_kernel_spmd(nc, in_maps, core_ids=list(range(NCORE)))
    shards = [res.results[cid]["out"].T for cid in range(NCORE)]
    return np.concatenate(shards, axis=0)[None].astype(np.float32)
